# version 04
# speedup vs baseline: 1.0144x; 1.0017x over previous
import contextlib
import numpy as np
import ml_dtypes
import concourse.bass as bass
import concourse.mybir as mybir
from concourse.alu_op_type import AluOpType as OP
from concourse.bass_utils import run_bass_kernel_spmd

F32 = mybir.dt.float32
BF16 = mybir.dt.bfloat16
AF = mybir.ActivationFunctionType
AX = mybir.AxisListType

NCORES = 8
T = 4096
D = 1024
NT = T // 128
DC = D // 128

GELU_C = 0.7978845608028654
GELU_A = 0.044715
MAGIC = 12582912.0



class Prog:
    def __init__(self, nc, st):
        self.nc = nc
        self.st = st
        self.dram = {}
        self._uid = 0

    def uid(self):
        self._uid += 1
        return self._uid

    def sb(self, name, shape, dtype):
        return self.st.enter_context(self.nc.sbuf_tensor(f"{name}{self.uid()}", shape, dtype))

    def ps(self, name, shape, dtype=F32):
        return self.st.enter_context(self.nc.psum_tensor(f"{name}{self.uid()}", shape, dtype))

    def sem(self, name):
        return self.st.enter_context(self.nc.semaphore(f"{name}{self.uid()}"))


class Stage:
    def __init__(self, name, ins, outs, build):
        self.name = name
        self.ins = ins
        self.outs = outs
        self.build = build


def np_to_bir(dt):
    dt = np.dtype(dt)
    if dt == np.float32:
        return F32
    if dt == ml_dtypes.bfloat16:
        return BF16
    raise ValueError(dt)


def v_pad(eng, junk):
    eng.tensor_copy(junk[:, :], junk[:, :])



def ln_vec(eng, scr, x_ap, out_ap, d, vsem, ssem, vval, sval, gb=None, bb=None,
           out2_ap=None):
    junk, m, vv, xc, sq = scr["junk"], scr["m"], scr["vv"], scr["xc"], scr["sq"]
    eng.tensor_reduce(m[:, :], x_ap, AX.X, OP.add)
    v_pad(eng, junk)
    eng.tensor_scalar(m[:, :], m[:, :], 1.0 / d, None, OP.mult)
    v_pad(eng, junk)
    eng.tensor_scalar(xc[:, 0:d], x_ap, m[:, :], None, OP.subtract)
    eng.tensor_tensor(sq[:, 0:d], xc[:, 0:d], xc[:, 0:d], OP.mult)
    eng.tensor_reduce(vv[:, :], sq[:, 0:d], AX.X, OP.add)
    v_pad(eng, junk)
    eng.tensor_scalar(vv[:, :], vv[:, :], 1.0 / d, 1e-5, OP.mult, OP.add)
    v_pad(eng, junk)
    eng.tensor_copy(junk[:, 0:1], junk[:, 0:1]).then_inc(vsem, 1)
    vval += 1
    sval += 1
    eng.wait_ge(ssem, sval)
    eng.reciprocal(scr["rstd"][:, :], scr["sd"][:, :])
    v_pad(eng, junk)
    if gb is None:
        eng.tensor_scalar(out_ap, xc[:, 0:d], scr["rstd"][:, :], None, OP.mult)
        if out2_ap is not None:
            eng.tensor_scalar(out2_ap, xc[:, 0:d], scr["rstd"][:, :], None, OP.mult)
    else:
        eng.tensor_scalar(scr["nrm"][:, 0:d], xc[:, 0:d], scr["rstd"][:, :], None, OP.mult)
        eng.tensor_tensor(scr["nrm"][:, 0:d], scr["nrm"][:, 0:d], gb[:, 0:d], OP.mult)
        eng.tensor_tensor(out_ap, scr["nrm"][:, 0:d], bb[:, 0:d], OP.add)
        if out2_ap is not None:
            eng.tensor_copy(out2_ap, out_ap)
    return vval, sval


def ln_sca(eng, scr, vsem, ssem, vval, sval):
    vval += 1
    eng.wait_ge(vsem, vval)
    eng.activation(scr["sd"][:, :], scr["vv"][:, :], AF.Sqrt)
    eng.activation(scr["junk"][:, 0:1], scr["sd"][:, 0:1], AF.Sign).then_inc(ssem, 1)
    sval += 1
    return vval, sval


def make_ln_scr(P, dmax):
    return {
        "junk": P.sb("junk", [128, 8], F32),
        "m": P.sb("lnm", [128, 1], F32),
        "vv": P.sb("lnv", [128, 1], F32),
        "sd": P.sb("lnsd", [128, 1], F32),
        "rstd": P.sb("lnr", [128, 1], F32),
        "xc": P.sb("lnxc", [128, dmax], F32),
        "sq": P.sb("lnsq", [128, dmax], F32),
        "nrm": P.sb("lnn", [128, dmax], F32),
    }



def make_q_scr(P, w):
    names = ["bmax", "scale", "rinv", "ax", "an", "r1", "r2", "qm", "s1m", "s2m",
             "d1", "d2", "qv", "sgs"]
    scr = {n: P.sb("q" + n, [128, w if n not in ("bmax", "scale", "rinv") else w // 16], F32)
           for n in names}
    scr["junk"] = P.sb("qjunk", [128, 8], F32)
    return scr


def emit_fp4_quant(eng, scr, src_ap, dst_ap, w):
    nb = w // 16
    bmax, scale, rinv = scr["bmax"], scr["scale"], scr["rinv"]
    ax, an, r1, r2, qm = scr["ax"], scr["an"], scr["r1"], scr["r2"], scr["qm"]
    s1m, s2m, d1, d2, qv, sgs = scr["s1m"], scr["s2m"], scr["d1"], scr["d2"], scr["qv"], scr["sgs"]
    src3 = src_ap.rearrange("p (b i) -> p b i", i=16)
    eng.tensor_reduce(bmax[:, 0:nb], src3, AX.X, OP.max, apply_absolute_value=True)
    v_pad(eng, scr["junk"])
    eng.tensor_scalar(scale[:, 0:nb], bmax[:, 0:nb], 1.0 / 6.0, 1e-12, OP.mult, OP.max)
    v_pad(eng, scr["junk"])
    eng.reciprocal(rinv[:, 0:nb], scale[:, 0:nb])
    v_pad(eng, scr["junk"])
    rb = rinv[:, 0:nb].unsqueeze(-1).broadcast_to([128, nb, 16])
    sb_ = scale[:, 0:nb].unsqueeze(-1).broadcast_to([128, nb, 16])
    eng.tensor_scalar(ax[:, 0:w], src_ap, -1.0, None, OP.mult)
    eng.tensor_tensor(ax[:, 0:w], ax[:, 0:w], src_ap, OP.max)
    an3 = an[:, 0:w].rearrange("p (b i) -> p b i", i=16)
    eng.tensor_tensor(an3, ax[:, 0:w].rearrange("p (b i) -> p b i", i=16), rb, OP.mult)
    a = an[:, 0:w]
    eng.tensor_scalar(r1[:, 0:w], a, 2.0, MAGIC, OP.mult, OP.add)
    eng.tensor_scalar(r1[:, 0:w], r1[:, 0:w], MAGIC, 0.5, OP.subtract, OP.mult)
    eng.tensor_scalar(r2[:, 0:w], a, MAGIC, MAGIC, OP.add, OP.subtract)
    eng.tensor_scalar(r1[:, 0:w], r1[:, 0:w], 2.0, None, OP.min)
    eng.tensor_scalar(qm[:, 0:w], r2[:, 0:w], 2.0, 4.0, OP.max, OP.min)
    eng.tensor_scalar(s1m[:, 0:w], a, 2.5, None, OP.is_gt)
    eng.tensor_scalar(s2m[:, 0:w], a, 4.5, None, OP.is_gt)
    eng.tensor_scalar(d2[:, 0:w], a, 5.0, None, OP.is_gt)
    eng.tensor_tensor(d1[:, 0:w], qm[:, 0:w], r1[:, 0:w], OP.subtract)
    eng.tensor_tensor(d1[:, 0:w], d1[:, 0:w], s1m[:, 0:w], OP.mult)
    eng.tensor_tensor(qv[:, 0:w], r1[:, 0:w], d1[:, 0:w], OP.add)
    eng.tensor_scalar(qm[:, 0:w], qm[:, 0:w], -1.0, 4.0, OP.mult, OP.add)
    eng.scalar_tensor_tensor(d2[:, 0:w], d2[:, 0:w], 2.0, qm[:, 0:w], OP.mult, OP.add)
    eng.tensor_tensor(d2[:, 0:w], d2[:, 0:w], s2m[:, 0:w], OP.mult)
    eng.tensor_tensor(qv[:, 0:w], qv[:, 0:w], d2[:, 0:w], OP.add)
    eng.tensor_scalar(sgs[:, 0:w], src_ap, 0.0, 2.0, OP.is_gt, OP.mult)
    eng.tensor_scalar(sgs[:, 0:w], sgs[:, 0:w], 1.0, None, OP.subtract)
    sg3 = sgs[:, 0:w].rearrange("p (b i) -> p b i", i=16)
    eng.tensor_tensor(sg3, sg3, sb_, OP.mult)
    eng.tensor_tensor(dst_ap.rearrange("p (b i) -> p b i", i=16),
                      qv[:, 0:w].rearrange("p (b i) -> p b i", i=16), sg3, OP.mult)



def _mk_const_inputs():
    bi = np.zeros((2048, 16), dtype=np.float32)
    for j in range(16):
        bi[j * 128:(j + 1) * 128, j] = 1.0
    return {
        "c_ident_bf": np.eye(128, dtype=ml_dtypes.bfloat16),
        "c_ident_f32": np.eye(128, dtype=np.float32),
        "c_triu_bf": np.triu(np.ones((128, 128))).astype(ml_dtypes.bfloat16),
        "c_blockind": bi,
    }


GROUPS = [
    ["cast", "l0_prep", "l0_qkv", "l0_attn", "l0_oproj", "l0_inproj",
     "l0_convscan", "l0_ffnout", "l0_mlp", "l0_gate"],
    ["l1_attnproj", "l1_attn", "l1_oproj", "l1_moe_a"],
    ["l1_moe_b"],
]


def run_stages(stages, env, cores):
    import sys, time, os
    by_name = {s.name: s for s in stages}
    groups = [[by_name[n] for n in g if n in by_name] for g in GROUPS]
    groups = [g for g in groups if g]
    assert sum(len(g) for g in groups) == len(stages)
    for gi, grp in enumerate(groups):
        t0 = time.time()
        gname = "+".join(s.name for s in grp)
        print(f"[group {gname}] building...", file=sys.stderr, flush=True)
        produced = set()
        for s in grp:
            produced |= set(s.outs)
        gins = []
        for s in grp:
            for name in s.ins:
                if name not in produced and name not in gins:
                    gins.append(name)
        st = contextlib.ExitStack()
        with st:
            nc = bass.Bass("TRN2", target_bir_lowering=False)
            P = Prog(nc, st)
            for name in gins:
                arr = env[name]
                a0 = arr[0] if isinstance(arr, list) else arr
                P.dram[name] = nc.dram_tensor(name, list(a0.shape), np_to_bir(a0.dtype),
                                              kind="ExternalInput")
            for s in grp:
                for name, (shape, dt) in s.outs.items():
                    P.dram[name] = nc.dram_tensor(name, list(shape), np_to_bir(dt),
                                                  kind="ExternalOutput")
            for s in grp:
                with nc.reset_on_exit():
                    with contextlib.ExitStack() as sub:
                        old = P.st
                        P.st = sub
                        s.build(P)
                        P.st = old
            in_maps = []
            for ci in range(cores):
                m = {}
                for name in gins:
                    arr = env[name]
                    m[name] = arr[ci] if isinstance(arr, list) else arr
                in_maps.append(m)
            off = int(os.environ.get("CORE_OFFSET", "0"))
            tr = os.environ.get("TRACE_STAGES") == "1"
            res = run_bass_kernel_spmd(nc, in_maps, core_ids=[c + off for c in range(cores)],
                                       trace=tr)
            if tr:
                print(f"[group {gname}] exec_time_ns={res.exec_time_ns}",
                      file=sys.stderr, flush=True)
        for s in grp:
            for name in s.outs:
                env[name] = [res.results[ci][name] for ci in range(cores)]
        print(f"[group {gname}] done in {time.time()-t0:.1f}s", file=sys.stderr, flush=True)
    return env


def kernel(**inputs):
    env = {}
    x = np.ascontiguousarray(inputs["L0_attn_x"], dtype=np.float32)
    env["x"] = [np.ascontiguousarray(x[c]) for c in range(NCORES)]
    for k, v in inputs.items():
        if k != "L0_attn_x":
            env[k] = np.ascontiguousarray(v)
    env.update(_mk_const_inputs())
    env = run_stages(build_all_stages(), env, NCORES)
    out = np.stack(env["x2"], axis=0)
    return out.astype(np.float32)



CAST_LIST = [
    ("L0_attn_Wq", (D, D)), ("L0_attn_bq", (D,)), ("L0_attn_Wk", (D, D)),
    ("L0_attn_bk", (D,)), ("L0_attn_Wv", (D, D)), ("L0_attn_bv", (D,)),
    ("L0_attn_Wo", (D, D)), ("L0_attn_bo", (D,)),
    ("L0_attn_in_w", (D, 4096)), ("L0_attn_in_b", (4096,)),
    ("L0_attn_xp_w", (2048, 32)),
    ("L0_attn_out_w", (2048, D)), ("L0_attn_out_b", (D,)),
    ("L0_attn_fc1_w", (D, 4096)), ("L0_attn_fc1_b", (4096,)),
    ("L0_attn_fc2_w", (4096, D)), ("L0_attn_fc2_b", (D,)),
    ("L1_attn_W1_q", (64, 2048)), ("L1_attn_W2_q", (64, 2048)),
    ("L1_attn_W1_k", (64, D)), ("L1_attn_W2_k", (64, D)),
    ("L1_attn_Wv", (D, D)), ("L1_attn_bv", (D,)),
    ("L1_attn_Wo", (D, D)), ("L1_attn_bo", (D,)),
    ("L1_ffn_W1", (8, D, 4096)), ("L1_ffn_b1", (8, 4096)),
    ("L1_ffn_W2", (8, 4096, D)), ("L1_ffn_b2", (8, D)),
]


def st_cast():
    ins = [n for n, _ in CAST_LIST]
    outs = {"cb_" + n: (s, ml_dtypes.bfloat16) for n, s in CAST_LIST}

    def build(P):
        nc = P.nc
        stw = P.sem("stw")
        with nc.Block() as blk:
            @blk.gpsimd
            def _(eng):
                cnt = 0
                for n, s in CAST_LIST:
                    src_ap = P.dram[n].ap()
                    dst_ap = P.dram["cb_" + n].ap()
                    if len(s) == 1:
                        src_ap = src_ap.unsqueeze(0)
                        dst_ap = dst_ap.unsqueeze(0)
                    elif len(s) == 3:
                        src_ap = src_ap.rearrange("a b c -> (a b) c")
                        dst_ap = dst_ap.rearrange("a b c -> (a b) c")
                    eng.dma_start(dst_ap, src_ap).then_inc(stw, 16)
                    cnt += 1
                eng.wait_ge(stw, 16 * cnt)

    return Stage("cast", ins, outs, build)


def build_all_stages():
    return [st_cast(), st_l0_prep(), st_l0_qkv(), st_l0_attn(), st_l0_oproj(),
            st_l0_inproj(), st_l0_convscan(), st_l0_ffnout(),
            st_l0_mlp(), st_l0_gate(), st_l1_attnproj(), st_l1_attn(),
            st_l1_oproj(), st_l1_moe('a'), st_l1_moe('b')]



def st_l0_prep():
    ins = ["x", "L0_attn_ln1_g", "L0_attn_ln1_b"]
    outs = {
        "xn": ((T, D), np.float32),
        "h1tm": ((T, D), ml_dtypes.bfloat16),
        "xqtm": ((T, D), ml_dtypes.bfloat16),
    }

    def build(P):
        nc = P.nc
        x_d, xn_d = P.dram["x"], P.dram["xn"]
        h1_d, xq_d = P.dram["h1tm"], P.dram["xqtm"]
        g_d, b_d = P.dram["L0_attn_ln1_g"], P.dram["L0_attn_ln1_b"]

        NB = 4
        xt = [P.sb("xt", [128, D], F32) for _ in range(NB)]
        h1t = [P.sb("h1t", [128, D], BF16) for _ in range(NB)]
        xqt = [P.sb("xqt", [128, D], BF16) for _ in range(NB)]
        xnt = [P.sb("xnt", [128, D], F32) for _ in range(NB)]
        scr = make_ln_scr(P, D)
        qscr = make_q_scr(P, D)
        gb = P.sb("gb", [128, D], F32)
        bb = P.sb("bb", [128, D], F32)
        grow = P.sb("grow", [1, D], F32)
        brow = P.sb("brow", [1, D], F32)
        ones1 = P.sb("ones1", [1, 512], F32)
        bcps = P.ps("bcps", [128, 512], F32)

        gsem = P.sem("gsem")
        ldb = [P.sem(f"ldb{i}") for i in range(NB)]
        stb = [P.sem(f"stb{i}") for i in range(NB)]
        vsem = P.sem("vs")
        ssem = P.sem("ss")
        tdone = P.sem("td")
        bcs = P.sem("bcs")

        with nc.Block() as blk:
            @blk.sync
            def _(eng):
                eng.dma_start(grow[:, :], g_d.ap().unsqueeze(0)).then_inc(gsem, 16)
                eng.dma_start(brow[:, :], b_d.ap().unsqueeze(0)).then_inc(gsem, 16)

            @blk.vector
            def _(eng):
                eng.memset(ones1[:, :], 1.0)
                eng.memset(scr["junk"][:, :], 0.0)
                eng.memset(qscr["junk"][:, :], 0.0)

        with nc.Block() as blk:
            @blk.tensor
            def _(eng):
                eng.wait_ge(gsem, 32)
                k = 0
                for srcrow in (grow, brow):
                    for j in range(2):
                        if k > 0:
                            eng.wait_ge(bcs, k)
                        eng.matmul(bcps[:, :], ones1[0:1, 0:128],
                                   srcrow[:, j * 512:(j + 1) * 512],
                                   start=True, stop=True).then_inc(bcs, 1)
                        k += 2

            @blk.vector
            def _(eng):
                k = 0
                for dst in (gb, bb):
                    for j in range(2):
                        eng.wait_ge(bcs, k + 1)
                        eng.tensor_copy(dst[:, j * 512:(j + 1) * 512],
                                        bcps[:, :]).then_inc(bcs, 1)
                        k += 2

        with nc.Block() as blk:
            @blk.sync
            def _(eng):
                def load(i):
                    b = i % NB
                    if i >= NB:
                        eng.wait_ge(tdone, i - NB + 1)
                    eng.dma_start(xt[b][:, :], x_d[i * 128:(i + 1) * 128, :]
                                  ).then_inc(ldb[b], 16)

                def store(i):
                    b = i % NB
                    eng.wait_ge(tdone, i + 1)
                    eng.dma_start(xn_d[i * 128:(i + 1) * 128, :], xnt[b][:, :]).then_inc(stb[b], 16)
                    eng.dma_start(h1_d[i * 128:(i + 1) * 128, :], h1t[b][:, :]).then_inc(stb[b], 16)
                    eng.dma_start(xq_d[i * 128:(i + 1) * 128, :], xqt[b][:, :]).then_inc(stb[b], 16)
                load(0)
                load(1)
                for i in range(NT):
                    if i + 2 < NT:
                        load(i + 2)
                    store(i)
                for b in range(NB):
                    eng.wait_ge(stb[b], 48 * len([t for t in range(NT) if t % NB == b]))

            @blk.scalar
            def _(eng):
                vv, sv = 0, 0
                for i in range(NT):
                    vv, sv = ln_sca(eng, scr, vsem, ssem, vv, sv)
                    vv, sv = ln_sca(eng, scr, vsem, ssem, vv, sv)

            @blk.vector
            def _(eng):
                vv, sv = 0, 0
                for i in range(NT):
                    b = i % NB
                    eng.wait_ge(ldb[b], 16 * (i // NB + 1))
                    if i >= NB:
                        eng.wait_ge(stb[b], 48 * ((i - NB) // NB + 1))
                    vv, sv = ln_vec(eng, scr, xt[b][:, :], xnt[b][:, :], D,
                                    vsem, ssem, vv, sv)
                    vv, sv = ln_vec(eng, scr, xnt[b][:, :], h1t[b][:, :], D,
                                    vsem, ssem, vv, sv, gb=gb, bb=bb)
                    emit_fp4_quant(eng, qscr, xnt[b][:, :], xqt[b][:, :], D)
                    eng.tensor_copy(scr["junk"][:, 0:1], xnt[b][:, 0:1]).then_inc(tdone, 1)

    return Stage("l0_prep", ins, outs, build)


def st_l0_qkv():
    ins = ["h1tm", "cb_L0_attn_Wq", "cb_L0_attn_bq", "cb_L0_attn_Wk", "cb_L0_attn_bk",
           "cb_L0_attn_Wv", "cb_L0_attn_bv"]
    outs = {
        "qfm": ((D, T), ml_dtypes.bfloat16),
        "kfm": ((D, T), ml_dtypes.bfloat16),
        "vaug": ((T, 1040), ml_dtypes.bfloat16),
    }

    def build(P):
        nc = P.nc
        h1fm = P.sb("h1fm", [128, DC * T], BF16)
        wq = P.sb("wq", [128, DC * D], BF16)
        wk = P.sb("wk", [128, DC * D], BF16)
        wv = P.sb("wv", [128, DC * D], BF16)
        bqrow = P.sb("bqrow", [1, D], BF16)
        bkrow = P.sb("bkrow", [1, D], BF16)
        bvrow = P.sb("bvrow", [1, D], BF16)
        ones512 = P.sb("ones512", [1, 512], BF16)
        ones1 = P.sb("ones1", [1, 128], BF16)
        stg = [P.sb("stg", [128, 512], BF16) for _ in range(3)]
        vstg = [P.sb("vstg", [128, 1040], BF16) for _ in range(3)]
        psA = [P.ps("psA", [128, 512], F32) for _ in range(2)]
        psV = [P.ps("psV", [128, 1024], F32) for _ in range(2)]

        ld = P.sem("ld")
        mms = P.sem("mms")
        evs = P.sem("evs")
        stb = [P.sem(f"stb{i}") for i in range(3)]
        mms2 = P.sem("mms2")
        evs2 = P.sem("evs2")
        stb2 = [P.sem(f"stb2{i}") for i in range(3)]

        with nc.Block() as blk:
            @blk.sync
            def _(eng):
                for kc in range(DC):
                    eng.dma_start(
                        h1fm[:, kc * T:(kc + 1) * T],
                        P.dram["h1tm"][:, kc * 128:(kc + 1) * 128],
                        transpose=True).then_inc(ld, 16)
                for w_sb, nm in ((wq, "cb_L0_attn_Wq"), (wk, "cb_L0_attn_Wk"),
                                 (wv, "cb_L0_attn_Wv")):
                    eng.dma_start(w_sb.ap().rearrange("p (kc n) -> p kc n", kc=DC),
                                  P.dram[nm].ap().rearrange("(kc p) n -> p kc n", p=128)
                                  ).then_inc(ld, 16)
                for row, nm in ((bqrow, "cb_L0_attn_bq"), (bkrow, "cb_L0_attn_bk"),
                                (bvrow, "cb_L0_attn_bv")):
                    eng.dma_start(row[:, :], P.dram[nm].ap().unsqueeze(0)).then_inc(ld, 16)

            @blk.vector
            def _(eng):
                eng.memset(ones512[:, :], 1.0)
                eng.memset(ones1[:, :], 1.0)
                for s in vstg:
                    eng.memset(s[:, :], 1.0)

        with nc.Block() as blk:
            tiles = [(which, nb, tw) for which in range(2) for nb in range(DC)
                     for tw in range(T // 512)]

            @blk.tensor
            def _(eng):
                eng.wait_ge(ld, 16 * 14)
                for n, (which, nb, tw) in enumerate(tiles):
                    w_sb = (wq, wk)[which]
                    brow = (bqrow, bkrow)[which]
                    if n >= 2:
                        eng.wait_ge(evs, n - 1)
                    for kc in range(DC):
                        eng.matmul(psA[n % 2][:, :],
                                   w_sb[:, kc * D + nb * 128: kc * D + (nb + 1) * 128],
                                   h1fm[:, kc * T + tw * 512: kc * T + tw * 512 + 512],
                                   start=(kc == 0), stop=False)
                    eng.matmul(psA[n % 2][:, :], brow[:, nb * 128:(nb + 1) * 128],
                               ones512[:, :], start=False, stop=True).then_inc(mms, 1)

            @blk.vector
            def _(eng):
                for n, (which, nb, tw) in enumerate(tiles):
                    eng.wait_ge(mms, n + 1)
                    if n >= 3:
                        eng.wait_ge(stb[n % 3], 16 * ((n - 3) // 3 + 1))
                    eng.tensor_copy(stg[n % 3][:, :], psA[n % 2][:, :]).then_inc(evs, 1)

            @blk.sync
            def _(eng):
                for n, (which, nb, tw) in enumerate(tiles):
                    eng.wait_ge(evs, n + 1)
                    dst = (P.dram["qfm"], P.dram["kfm"])[which]
                    eng.dma_start(dst[nb * 128:(nb + 1) * 128, tw * 512:tw * 512 + 512],
                                  stg[n % 3][:, :]).then_inc(stb[n % 3], 16)
                for b in range(3):
                    eng.wait_ge(stb[b], 16 * len([t for t in range(len(tiles)) if t % 3 == b]))

        with nc.Block() as blk:
            @blk.tensor
            def _(eng):
                for tt in range(NT):
                    if tt >= 2:
                        eng.wait_ge(evs2, tt - 1)
                    for nh in range(2):
                        for kc in range(DC):
                            eng.matmul(psV[tt % 2][:, nh * 512:(nh + 1) * 512],
                                       h1fm[:, kc * T + tt * 128: kc * T + (tt + 1) * 128],
                                       wv[:, kc * D + nh * 512: kc * D + (nh + 1) * 512],
                                       start=(kc == 0), stop=False)
                        mmi = eng.matmul(psV[tt % 2][:, nh * 512:(nh + 1) * 512],
                                         ones1[:, :], bvrow[:, nh * 512:(nh + 1) * 512],
                                         start=False, stop=True)
                    mmi.then_inc(mms2, 1)

            @blk.vector
            def _(eng):
                for tt in range(NT):
                    eng.wait_ge(mms2, tt + 1)
                    if tt >= 3:
                        eng.wait_ge(stb2[tt % 3], 16 * ((tt - 3) // 3 + 1))
                    for h in range(16):
                        eng.tensor_copy(vstg[tt % 3][:, h * 65:h * 65 + 64],
                                        psV[tt % 2][:, h * 64:(h + 1) * 64])
                    eng.tensor_copy(vstg[tt % 3][:, 1039:1040],
                                    vstg[tt % 3][:, 1039:1040]).then_inc(evs2, 1)

            @blk.sync
            def _(eng):
                for tt in range(NT):
                    eng.wait_ge(evs2, tt + 1)
                    eng.dma_start(P.dram["vaug"][tt * 128:(tt + 1) * 128, :],
                                  vstg[tt % 3][:, :]).then_inc(stb2[tt % 3], 16)
                for b in range(3):
                    eng.wait_ge(stb2[b], 16 * len([t for t in range(NT) if t % 3 == b]))

    return Stage("l0_qkv", ins, outs, build)



def st_l0_attn():
    ins = ["qfm", "kfm", "vaug", "c_triu_bf"]
    outs = {"attn_tm": ((T, D), ml_dtypes.bfloat16)}

    def build(P):
        nc = P.nc
        vaug = P.sb("vaug", [128, NT * 1040], BF16)
        qh = [P.sb("qh", [64, T], BF16) for _ in range(2)]
        kh = [P.sb("kh", [64, T], BF16) for _ in range(2)]
        triu = P.sb("triu", [128, 128], BF16)
        pt = [P.sb("pt", [128, 512], BF16) for _ in range(2)]
        attn = P.sb("attn", [128, NT * D], BF16)
        rl = P.sb("rl", [128, 1], F32)
        junk = P.sb("junk", [128, 8], F32)
        psS = [P.ps("psS", [128, 512], F32) for _ in range(2)]
        psO = [P.ps("psO", [128, 512], F32) for _ in range(4)]

        vg = P.sem("vg")
        qks = [P.sem(f"qks{i}") for i in range(2)]
        sst = P.sem("sst")
        hs = P.sem("hs")
        mmS = P.sem("mmS")
        exS = P.sem("exS")
        mkS = P.sem("mkS")
        pvS = P.sem("pvS")
        oeS = P.sem("oeS")

        sched = []
        for h in range(16):
            for qg in range(8):
                for kc in range(4 * (qg + 1)):
                    sched.append((h, qg, kc))
        nch = len(sched)
        ngrp = 16 * 8

        with nc.Block() as blk:
            @blk.sync
            def _(eng):
                eng.dma_start(triu[:, :], P.dram["c_triu_bf"][:, :]).then_inc(vg, 16)
                for tt in range(NT):
                    eng.dma_start(vaug[:, tt * 1040:(tt + 1) * 1040],
                                  P.dram["vaug"][tt * 128:(tt + 1) * 128, :]).then_inc(vg, 16)
                for h in range(16):
                    if h >= 2:
                        eng.wait_ge(hs, h - 1)
                    eng.dma_start(qh[h % 2][:, :],
                                  P.dram["qfm"][h * 64:(h + 1) * 64, :]).then_inc(qks[h % 2], 16)
                    eng.dma_start(kh[h % 2][:, :],
                                  P.dram["kfm"][h * 64:(h + 1) * 64, :]).then_inc(qks[h % 2], 16)

            @blk.tensor
            def _(eng):
                nmm = 0
                npv = 0
                for n, (h, qg, kc) in enumerate(sched):
                    last_in_grp = (kc == 4 * (qg + 1) - 1)
                    if kc == 0 and qg == 0:
                        if h == 0:
                            eng.wait_ge(vg, 16 * (1 + NT))
                        eng.wait_ge(qks[h % 2], 32 * (h // 2 + 1))
                    if nmm >= 2:
                        eng.wait_ge(exS, nmm - 1)
                    eng.matmul(psS[nmm % 2][:, :],
                               kh[h % 2][:, kc * 128:(kc + 1) * 128],
                               qh[h % 2][:, qg * 512:(qg + 1) * 512],
                               start=True, stop=True).then_inc(mmS, 1)
                    nmm += 1
                    gi = h * 8 + qg
                    if gi >= 1 and kc == 0:
                        eng.wait_ge(oeS, gi)
                    eng.wait_ge(mkS, nmm)
                    last_mm = None
                    for qt in range(4):
                        ktile = kc
                        qtile = 4 * qg + qt
                        if ktile > qtile:
                            continue
                        last_mm = eng.matmul(
                            psO[qt][:, 0:65],
                            pt[nmm % 2][:, qt * 128:(qt + 1) * 128],
                            vaug[:, ktile * 1040 + h * 65: ktile * 1040 + (h + 1) * 65],
                            start=(kc == 0), stop=last_in_grp)
                    last_mm.then_inc(pvS, 1)
                    npv += 1
                    if kc == 0 and qg == 7:
                        pass

            @blk.scalar
            def _(eng):
                for n, (h, qg, kc) in enumerate(sched):
                    eng.wait_ge(mmS, n + 1)
                    if n >= 2:
                        eng.wait_ge(pvS, n - 1)
                    eng.activation(pt[(n + 1) % 2][:, :], psS[n % 2][:, :], AF.Exp,
                                   scale=0.125).then_inc(exS, 1)

            @blk.vector
            def _(eng):
                gdone = 0
                for n, (h, qg, kc) in enumerate(sched):
                    last_in_grp = (kc == 4 * (qg + 1) - 1)
                    eng.wait_ge(exS, n + 1)
                    did = False
                    for qt in range(4):
                        if kc == 4 * qg + qt:
                            eng.tensor_tensor(pt[(n + 1) % 2][:, qt * 128:(qt + 1) * 128],
                                              pt[(n + 1) % 2][:, qt * 128:(qt + 1) * 128],
                                              triu[:, :], OP.mult)
                            did = True
                    if not did:
                        v_pad(eng, junk)
                    eng.tensor_copy(junk[:, 0:1], junk[:, 0:1]).then_inc(mkS, 1)
                    if last_in_grp:
                        gi = h * 8 + qg
                        eng.wait_ge(pvS, n + 1)
                        for qt in range(4):
                            qtile = 4 * qg + qt
                            eng.reciprocal(rl[:, :], psO[qt][:, 64:65])
                            v_pad(eng, junk)
                            eng.tensor_scalar(
                                attn[:, qtile * D + h * 64: qtile * D + (h + 1) * 64],
                                psO[qt][:, 0:64],
                                rl[:, :], None, OP.mult)
                        eng.tensor_copy(junk[:, 0:1], junk[:, 0:1]).then_inc(oeS, 1)
                        gdone += 1
                        if qg == 7:
                            eng.tensor_copy(junk[:, 0:1], junk[:, 0:1]).then_inc(hs, 1)

        with nc.Block() as blk:
            @blk.sync
            def _(eng):
                for tt in range(NT):
                    eng.dma_start(P.dram["attn_tm"][tt * 128:(tt + 1) * 128, :],
                                  attn[:, tt * D:(tt + 1) * D]).then_inc(sst, 16)
                eng.wait_ge(sst, 16 * NT)

    return Stage("l0_attn", ins, outs, build)



def st_l0_oproj():
    ins = ["attn_tm", "xn", "cb_L0_attn_Wo", "cb_L0_attn_bo", "L0_attn_ln2_g", "L0_attn_ln2_b"]
    outs = {
        "a1": ((T, D), np.float32),
        "h2tm": ((T, D), ml_dtypes.bfloat16),
    }

    def build(P):
        nc = P.nc
        afm = P.sb("afm", [128, DC * T], BF16)
        wo = P.sb("wo", [128, DC * D], BF16)
        borow = P.sb("borow", [1, D], BF16)
        ones1 = P.sb("ones1", [1, 128], BF16)
        xnt = [P.sb("xnt", [128, D], F32) for _ in range(3)]
        a1t = [P.sb("a1t", [128, D], F32) for _ in range(3)]
        h2t = [P.sb("h2t", [128, D], BF16) for _ in range(3)]
        gb = P.sb("gb", [128, D], F32)
        bb = P.sb("bb", [128, D], F32)
        grow = P.sb("grow", [1, D], F32)
        brow = P.sb("brow", [1, D], F32)
        onesf = P.sb("onesf", [1, 128], F32)
        scr = make_ln_scr(P, D)
        psA = [P.ps("psA", [128, 1024], F32) for _ in range(2)]
        psB = P.ps("psB", [128, 512], F32)

        ld = P.sem("ld")
        gsem = P.sem("gsem")
        xlb = [P.sem(f"xlb{i}") for i in range(3)]
        stb = [P.sem(f"stb{i}") for i in range(3)]
        mms = P.sem("mms")
        evs = P.sem("evs")
        vsem = P.sem("vs")
        ssem = P.sem("ss")
        bcs = P.sem("bcs")

        with nc.Block() as blk:
            @blk.sync
            def _(eng):
                for kc in range(DC):
                    eng.dma_start(afm[:, kc * T:(kc + 1) * T],
                                  P.dram["attn_tm"][:, kc * 128:(kc + 1) * 128],
                                  transpose=True).then_inc(ld, 16)
                eng.dma_start(wo.ap().rearrange("p (kc n) -> p kc n", kc=DC),
                              P.dram["cb_L0_attn_Wo"].ap().rearrange("(kc p) n -> p kc n", p=128)
                              ).then_inc(ld, 16)
                eng.dma_start(borow[:, :], P.dram["cb_L0_attn_bo"].ap().unsqueeze(0)).then_inc(ld, 16)
                eng.dma_start(grow[:, :], P.dram["L0_attn_ln2_g"].ap().unsqueeze(0)).then_inc(gsem, 16)
                eng.dma_start(brow[:, :], P.dram["L0_attn_ln2_b"].ap().unsqueeze(0)).then_inc(gsem, 16)

            @blk.vector
            def _(eng):
                eng.memset(ones1[:, :], 1.0)
                eng.memset(onesf[:, :], 1.0)
                eng.memset(scr["junk"][:, :], 0.0)

        with nc.Block() as blk:
            @blk.tensor
            def _(eng):
                eng.wait_ge(gsem, 32)
                k = 0
                for src in (grow, brow):
                    for j in range(2):
                        if k > 0:
                            eng.wait_ge(bcs, k)
                        eng.matmul(psB[:, :], onesf[0:1, 0:128],
                                   src[:, j * 512:(j + 1) * 512],
                                   start=True, stop=True).then_inc(bcs, 1)
                        k += 2

            @blk.vector
            def _(eng):
                k = 0
                for dst in (gb, bb):
                    for j in range(2):
                        eng.wait_ge(bcs, k + 1)
                        eng.tensor_copy(dst[:, j * 512:(j + 1) * 512], psB[:, :]).then_inc(bcs, 1)
                        k += 2

        with nc.Block() as blk:
            @blk.sync
            def _(eng):
                def load(tt):
                    b = tt % 3
                    if tt >= 3:
                        eng.wait_ge(evs, tt - 2)
                        eng.wait_ge(stb[b], 32 * ((tt - 3) // 3 + 1))
                    eng.dma_start(xnt[b][:, :],
                                  P.dram["xn"][tt * 128:(tt + 1) * 128, :]).then_inc(xlb[b], 16)

                def store(tt):
                    b = tt % 3
                    eng.wait_ge(evs, tt + 1)
                    eng.dma_start(P.dram["a1"][tt * 128:(tt + 1) * 128, :],
                                  a1t[b][:, :]).then_inc(stb[b], 16)
                    eng.dma_start(P.dram["h2tm"][tt * 128:(tt + 1) * 128, :],
                                  h2t[b][:, :]).then_inc(stb[b], 16)
                load(0)
                load(1)
                for tt in range(NT):
                    if tt + 2 < NT:
                        load(tt + 2)
                    store(tt)
                for b in range(3):
                    eng.wait_ge(stb[b], 32 * len([t for t in range(NT) if t % 3 == b]))

            @blk.tensor
            def _(eng):
                eng.wait_ge(ld, 16 * (DC + 2))
                for tt in range(NT):
                    if tt >= 2:
                        eng.wait_ge(evs, tt - 1)
                    for nh in range(2):
                        for kc in range(DC):
                            eng.matmul(psA[tt % 2][:, nh * 512:(nh + 1) * 512],
                                       afm[:, kc * T + tt * 128: kc * T + (tt + 1) * 128],
                                       wo[:, kc * D + nh * 512: kc * D + (nh + 1) * 512],
                                       start=(kc == 0), stop=False)
                        mmi = eng.matmul(psA[tt % 2][:, nh * 512:(nh + 1) * 512],
                                         ones1[:, :], borow[:, nh * 512:(nh + 1) * 512],
                                         start=False, stop=True)
                    mmi.then_inc(mms, 1)

            @blk.scalar
            def _(eng):
                vv, sv = 0, 0
                for tt in range(NT):
                    vv, sv = ln_sca(eng, scr, vsem, ssem, vv, sv)

            @blk.vector
            def _(eng):
                vv, sv = 0, 0
                for tt in range(NT):
                    b = tt % 3
                    eng.wait_ge(mms, tt + 1)
                    eng.wait_ge(xlb[b], 16 * (tt // 3 + 1))
                    if tt >= 3:
                        eng.wait_ge(stb[b], 32 * ((tt - 3) // 3 + 1))
                    eng.tensor_tensor(a1t[b][:, :], psA[tt % 2][:, :], xnt[b][:, :], OP.add)
                    vv, sv = ln_vec(eng, scr, a1t[b][:, :], h2t[b][:, :], D,
                                    vsem, ssem, vv, sv, gb=gb, bb=bb)
                    eng.tensor_copy(scr["junk"][:, 0:1], a1t[b][:, 0:1]).then_inc(evs, 1)

    return Stage("l0_oproj", ins, outs, build)



def st_l0_inproj():
    ins = ["h2tm", "cb_L0_attn_in_w", "cb_L0_attn_in_b"]
    outs = {
        "xmfm": ((2048, 4112), np.float32),
        "resfm": ((2048, T), ml_dtypes.bfloat16),
    }

    def build(P):
        nc = P.nc
        h2fm = P.sb("h2fm", [128, DC * T], BF16)
        wi = P.sb("wi", [128, DC * 4096], BF16)
        birow = P.sb("birow", [1, 4096], BF16)
        ones512 = P.sb("ones512", [1, 512], BF16)
        stgf = [P.sb("stgf", [128, 512], F32) for _ in range(3)]
        stgb = [P.sb("stgb", [128, 512], BF16) for _ in range(3)]
        zcol = P.sb("zcol", [128, 16], F32)
        psA = [P.ps("psA", [128, 512], F32) for _ in range(2)]
        ld = P.sem("ld")
        zs = P.sem("zs")
        mms = P.sem("mms")
        evs = P.sem("evs")
        stb = [P.sem(f"stb{i}") for i in range(3)]

        with nc.Block() as blk:
            @blk.sync
            def _(eng):
                for kc in range(DC):
                    eng.dma_start(h2fm[:, kc * T:(kc + 1) * T],
                                  P.dram["h2tm"][:, kc * 128:(kc + 1) * 128],
                                  transpose=True).then_inc(ld, 16)
                eng.dma_start(wi.ap().rearrange("p (kc n) -> p kc n", kc=DC),
                              P.dram["cb_L0_attn_in_w"].ap().rearrange("(kc p) n -> p kc n", p=128)
                              ).then_inc(ld, 16)
                eng.dma_start(birow[:, :], P.dram["cb_L0_attn_in_b"].ap().unsqueeze(0)
                              ).then_inc(ld, 16)

            @blk.vector
            def _(eng):
                eng.memset(ones512[:, :], 1.0)
                eng.memset(zcol[:, :], 0.0)

        with nc.Block() as blk:
            tiles = [(ft, tw) for ft in range(32) for tw in range(T // 512)]

            @blk.sync
            def _(eng):
                for ft in range(16):
                    eng.dma_start(P.dram["xmfm"][ft * 128:(ft + 1) * 128, 0:16],
                                  zcol[:, :]).then_inc(zs, 16)
                for n, (ft, tw) in enumerate(tiles):
                    eng.wait_ge(evs, n + 1)
                    if ft < 16:
                        eng.dma_start(
                            P.dram["xmfm"][ft * 128:(ft + 1) * 128,
                                           16 + tw * 512:16 + tw * 512 + 512],
                            stgf[n % 3][:, :]).then_inc(stb[n % 3], 16)
                    else:
                        eng.dma_start(
                            P.dram["resfm"][(ft - 16) * 128:(ft - 15) * 128,
                                            tw * 512:tw * 512 + 512],
                            stgb[n % 3][:, :]).then_inc(stb[n % 3], 16)
                for b in range(3):
                    eng.wait_ge(stb[b], 16 * len([t for t in range(len(tiles)) if t % 3 == b]))
                eng.wait_ge(zs, 16 * 16)

            @blk.tensor
            def _(eng):
                eng.wait_ge(ld, 16 * (DC + 2))
                for n, (ft, tw) in enumerate(tiles):
                    if n >= 2:
                        eng.wait_ge(evs, n - 1)
                    for kc in range(DC):
                        eng.matmul(psA[n % 2][:, :],
                                   wi[:, kc * 4096 + ft * 128: kc * 4096 + (ft + 1) * 128],
                                   h2fm[:, kc * T + tw * 512: kc * T + tw * 512 + 512],
                                   start=(kc == 0), stop=False)
                    eng.matmul(psA[n % 2][:, :], birow[:, ft * 128:(ft + 1) * 128],
                               ones512[:, :], start=False, stop=True).then_inc(mms, 1)

            @blk.vector
            def _(eng):
                for n, (ft, tw) in enumerate(tiles):
                    eng.wait_ge(mms, n + 1)
                    if n >= 3:
                        eng.wait_ge(stb[n % 3], 16 * ((n - 3) // 3 + 1))
                    dst = stgf[n % 3] if ft < 16 else stgb[n % 3]
                    eng.tensor_copy(dst[:, :], psA[n % 2][:, :]).then_inc(evs, 1)

    return Stage("l0_inproj", ins, outs, build)



def st_l0_convscan():
    ins = ["xmfm", "L0_attn_conv_w", "L0_attn_conv_b", "cb_L0_attn_xp_w", "L0_attn_xp_b",
           "L0_attn_dt", "L0_attn_A", "c_ident_f32"]
    outs = {"hs": ((16, T), np.float32), "hsbf": ((16, T), ml_dtypes.bfloat16),
    }

    def build(P):
        nc = P.nc
        NFT = 16
        xmt = [P.sb("xmt", [128, 4112], F32) for _ in range(2)]
        accb = [P.sb("accb", [128, T], F32) for _ in range(2)]
        xm2 = [P.sb("xm2", [128, T], BF16) for _ in range(2)]
        cvw = P.sb("cvw", [128, NFT * 4], F32)
        cvb = P.sb("cvb", [128, NFT], F32)
        xpw = P.sb("xpw", [128, NFT * 64], BF16)
        xpb = P.sb("xpb", [128, 64], F32)
        xdbl = P.sb("xdbl", [128, 2048], F32)
        idn = P.sb("idn", [128, 128], F32)
        bcT = P.sb("bcT", [64, T], F32)
        hsT = P.sb("hsT", [16, T + 80], F32)
        hsbf = P.sb("hsbf", [16, T], BF16)
        junk2 = P.sb("junk2", [16, 8], F32)
        dtc = P.sb("dtc", [16, 1], F32)
        dte = P.sb("dte", [16, 1], F32)
        at = P.sb("at", [16, 16], F32)
        atT = P.sb("atT", [16, 16], F32)
        dterow = P.sb("dterow", [1, 16], F32)
        d16 = P.sb("d16", [16, 16], F32)
        atmp = [P.sb("atmp", [16, 16], F32) for _ in range(2)]
        aexp = P.sb("aexp", [16, 16 * 16], F32)
        aexpb = P.sb("aexpb", [16, 16 * 16], BF16)
        hsTb = P.sb("hsTb", [16, T + 80], BF16)
        th = [P.sb("th", [16, 1], F32) for _ in range(2)]
        tmp2 = P.sb("tmp2", [16, 1], F32)
        junk = P.sb("junk", [128, 8], F32)
        onesf = P.sb("onesf", [1, 128], F32)
        xpbrow = P.sb("xpbrow", [1, 64], F32)
        psX = P.ps("psX", [128, 2048], F32)
        psT = P.ps("psT", [128, 512], F32)
        psU = [P.ps("psU", [16, 512], F32) for _ in range(2)]

        ld = P.sem("ld")
        aps = P.sem("aps")
        apv = P.sem("apv")
        apx = P.sem("apx")
        cnv = P.sem("cnv")
        sil = P.sem("sil")
        xmm = P.sem("xmm")
        xad = P.sem("xad")
        tpm = P.sem("tpm")
        tpd = P.sem("tpd")
        msem = P.sem("msem")
        ssem = P.sem("ssem")
        vdone = P.sem("vdone")
        stw = P.sem("stw")

        msem = P.sem("msem0")
        xlb = [P.sem(f"xlb{i}") for i in range(2)]
        with nc.Block() as blk:
            @blk.vector
            def _(eng):
                eng.memset(xpw[:, :], 0.0)
                eng.memset(onesf[:, :], 1.0)
                eng.memset(junk[:, :], 0.0)
                eng.memset(hsT[:, 0:1], 0.0)
                eng.memset(hsTb[:, 0:1], 0.0)
                eng.memset(xpbrow[:, 16:32], 0.0)
                eng.memset(xpbrow[:, 48:64], 0.0).then_inc(msem, 1)

            @blk.sync
            def _(eng):
                eng.dma_start(cvw.ap().rearrange("p (f j) -> p f j", j=4),
                              P.dram["L0_attn_conv_w"].ap()
                              .rearrange("(f p) o j -> p f (o j)", p=128)).then_inc(ld, 16)
                eng.dma_start(dtc[:, :], P.dram["L0_attn_dt"].ap().unsqueeze(-1)[0:16, :],
                              allow_slow_non_contiguous=True).then_inc(ld, 16)
                eng.dma_start(at[:, :], P.dram["L0_attn_A"][:, :]).then_inc(ld, 16)
                eng.dma_start(idn[:, :], P.dram["c_ident_f32"][:, :]).then_inc(ld, 16)
                eng.dma_start(xpbrow[:, 0:16],
                              P.dram["L0_attn_xp_b"].ap().unsqueeze(0)[:, 0:16]).then_inc(ld, 16)
                eng.dma_start(xpbrow[:, 32:48],
                              P.dram["L0_attn_xp_b"].ap().unsqueeze(0)[:, 16:32]).then_inc(ld, 16)
                eng.dma_start(cvb[:, :],
                              P.dram["L0_attn_conv_b"].ap().rearrange("(f p) -> p f", p=128),
                              allow_slow_non_contiguous=True).then_inc(ld, 16)
                eng.wait_ge(msem, 1)
                eng.dma_start(xpw.ap().rearrange("p (f c) -> p f c", c=64)[:, :, 0:16],
                              P.dram["cb_L0_attn_xp_w"].ap()
                              .rearrange("(f p) c -> p f c", p=128)[:, :, 0:16]).then_inc(ld, 16)
                eng.dma_start(xpw.ap().rearrange("p (f c) -> p f c", c=64)[:, :, 32:48],
                              P.dram["cb_L0_attn_xp_w"].ap()
                              .rearrange("(f p) c -> p f c", p=128)[:, :, 16:32]).then_inc(ld, 16)

        if CS_MODE == 3:
            with nc.Block() as blk:
                @blk.vector
                def _(eng):
                    eng.memset(aexp[:, :], 0.07)
                    eng.memset(bcT[:, :], 0.1)
                    eng.tensor_copy(junk[0:1, 0:1], junk[0:1, 0:1]).then_inc(apx, 16)
        if CS_MODE != 3:
          with nc.Block() as blk:
            @blk.scalar
            def _(eng):
                eng.wait_ge(ld, 16 * 9)
                eng.activation(dte[:, :], dtc[:, :], AF.Exp).then_inc(aps, 1)
                for i in range(16):
                    eng.wait_ge(apv, i + 1)
                    eng.activation(aexp[:, i * 16:(i + 1) * 16], atmp[i % 2][:, :],
                                   AF.Exp).then_inc(apx, 1)

            @blk.tensor
            def _(eng):
                eng.wait_ge(aps, 1)
                eng.transpose(psT[0:16, 0:16], at[:, :], idn[0:16, 0:16])
                eng.transpose(psT[0:1, 16:32], dte[:, :], idn[0:16, 0:16]).then_inc(aps, 1)
                eng.wait_ge(aps, 3)
                eng.matmul(psT[0:16, 32:48], onesf[0:1, 0:16], dterow[:, :],
                           start=True, stop=True).then_inc(aps, 1)
                eng.wait_ge(aps, 5)
                eng.matmul(psT[:, 64:128], onesf[0:1, :], xpbrow[:, :],
                           start=True, stop=True).then_inc(aps, 1)

            @blk.vector
            def _(eng):
                eng.wait_ge(aps, 2)
                eng.tensor_copy(atT[:, :], psT[0:16, 0:16])
                eng.tensor_copy(dterow[:, :], psT[0:1, 16:32]).then_inc(aps, 1)
                eng.wait_ge(aps, 4)
                eng.tensor_copy(d16[:, :], psT[0:16, 32:48]).then_inc(aps, 1)
                eng.wait_ge(aps, 6)
                eng.tensor_copy(xpb[:, :], psT[:, 64:128])
                v_pad(eng, junk)
                for i in range(16):
                    if i >= 2:
                        eng.wait_ge(apx, i - 1)
                    eng.tensor_scalar(atmp[i % 2][:, :], atT[:, :], d16[:, i:i + 1],
                                      None, OP.mult).then_inc(apv, 1)

        if CS_MODE == 1:
            with nc.Block() as blk:
                @blk.vector
                def _(eng):
                    eng.wait_ge(apx, 16)
                    eng.memset(bcT[:, :], 0.1)
        if CS_MODE == 0:
          with nc.Block() as blk:
            @blk.sync
            def _(eng):
                for ft in range(NFT):
                    if ft >= 2:
                        eng.wait_ge(cnv, ft - 1)
                    eng.dma_start(xmt[ft % 2][:, :],
                                  P.dram["xmfm"][ft * 128:(ft + 1) * 128, :]).then_inc(xlb[ft % 2], 16)

            @blk.vector
            def _(eng):
                eng.wait_ge(apx, 16)
                eng.tensor_copy(aexpb[:, :], aexp[:, :])
                def acc_psx(ft_prev):
                    eng.wait_ge(xmm, ft_prev + 1)
                    if ft_prev == 0:
                        eng.tensor_copy(xdbl[:, :], psX[:, :]).then_inc(xad, 1)
                    else:
                        eng.tensor_tensor(xdbl[:, :], xdbl[:, :], psX[:, :],
                                          OP.add).then_inc(xad, 1)
                for ft in range(NFT):
                    if ft >= 1:
                        acc_psx(ft - 1)
                    eng.wait_ge(xlb[ft % 2], 16 * (ft // 2 + 1))
                    if ft >= 2:
                        eng.wait_ge(sil, ft - 1)
                    x = xmt[ft % 2]
                    eng.tensor_scalar(accb[ft % 2][:, :], x[:, 13:13 + T],
                                      cvw[:, ft * 4:ft * 4 + 1], None, OP.mult)
                    for j in range(1, 4):
                        eng.scalar_tensor_tensor(accb[ft % 2][:, :], x[:, 13 + j:13 + j + T],
                                                 cvw[:, ft * 4 + j:ft * 4 + j + 1],
                                                 accb[ft % 2][:, :], OP.mult, OP.add)
                    eng.tensor_copy(junk[:, 0:1], junk[:, 0:1]).then_inc(cnv, 1)
                acc_psx(NFT - 1)
                x3 = xdbl.ap().rearrange("p (t c) -> p t c", c=64)
                b3 = xpb.ap().unsqueeze(1).broadcast_to([128, 32, 64])
                eng.tensor_tensor(x3, x3, b3, OP.add).then_inc(xad, 1)
                for tt in range(NT):
                    eng.wait_ge(tpm, tt + 1)
                    eng.tensor_copy(bcT[:, tt * 128:(tt + 1) * 128],
                                    psT[0:64, 0:128]).then_inc(tpd, 1)

            @blk.scalar
            def _(eng):
                for ft in range(NFT):
                    eng.wait_ge(cnv, ft + 1)
                    if ft >= 2:
                        eng.wait_ge(xmm, ft - 1)
                    eng.activation(xm2[ft % 2][:, :], accb[ft % 2][:, :], AF.Silu,
                                   bias=cvb[:, ft:ft + 1]).then_inc(sil, 1)

            @blk.tensor
            def _(eng):
                for ft in range(NFT):
                    eng.wait_ge(sil, ft + 1)
                    if ft >= 1:
                        eng.wait_ge(xad, ft)
                    mmi = None
                    for tt in range(NT):
                        mmi = eng.matmul(psX[:, tt * 64:(tt + 1) * 64],
                                         xm2[ft % 2][:, tt * 128:(tt + 1) * 128],
                                         xpw[:, ft * 64:(ft + 1) * 64],
                                         start=True, stop=True)
                    mmi.then_inc(xmm, 1)
                eng.wait_ge(xad, NFT + 1)
                for tt in range(NT):
                    if tt >= 1:
                        eng.wait_ge(tpd, tt)
                    eng.transpose(psT[0:64, 0:128], xdbl[:, tt * 64:(tt + 1) * 64],
                                  idn[:, :]).then_inc(tpm, 1)

        with nc.Block() as blk:
            @blk.tensor
            def _(eng):
                for t in range(T):
                    if t >= 1:
                        eng.wait_ge(vdone, t)
                        eng.wait_ge(ssem, t)
                    eng.matmul(psU[0][:, 0:2], aexpb[:, (t % 16) * 16:(t % 16) * 16 + 16],
                               hsTb[:, t:t + 2], start=True, stop=True).then_inc(msem, 1)

            @blk.scalar
            def _(eng):
                for t in range(T):
                    eng.wait_ge(msem, t + 1)
                    if t >= 1:
                        eng.wait_ge(vdone, t)
                    eng.activation(junk2[:, 0:1], junk2[:, 0:1], AF.Sign)
                    eng.activation(th[0][:, :], psU[0][:, 0:1], AF.Tanh)
                    eng.activation(junk2[:, 1:2], junk2[:, 1:2],
                                   AF.Sign).then_inc(ssem, 1)

            @blk.vector
            def _(eng):
                for t in range(T):
                    eng.wait_ge(ssem, t + 1)
                    v_pad(eng, junk)
                    v_pad(eng, junk)
                    eng.scalar_tensor_tensor(tmp2[:, :], hsT[:, t:t + 1],
                                             bcT[0:16, t:t + 1],
                                             th[0][:, :],
                                             OP.mult, OP.add)
                    eng.tensor_scalar(hsT[:, t + 1:t + 2], tmp2[:, :],
                                      bcT[32:48, t:t + 1], None, OP.add)
                    eng.tensor_copy(hsTb[:, t + 1:t + 2], hsT[:, t + 1:t + 2]).then_inc(vdone, 1)
                eng.tensor_copy(hsbf[:, :], hsT[:, 1:T + 1]).then_inc(vdone, 1)

            @blk.sync
            def _(eng):
                eng.wait_ge(vdone, T + 1)
                eng.dma_start(P.dram["hs"][:, :], hsT[:, 1:T + 1]).then_inc(stw, 16)
                eng.dma_start(P.dram["hsbf"][:, :], hsbf[:, :]).then_inc(stw, 16)
                eng.wait_ge(stw, 32)

    return Stage("l0_convscan", ins, outs, build)



def st_l0_ffnout():
    ins = ["hs", "resfm", "a1", "cb_L0_attn_out_w", "cb_L0_attn_out_b",
           "L0_attn_out_w", "L0_attn_ln3_g", "L0_attn_ln3_b", "c_blockind"]
    outs = {
        "a2": ((T, D), np.float32),
        "h3tm": ((T, D), ml_dtypes.bfloat16),
    }

    def build(P):
        nc = P.nc
        NJ = 16
        NW = T // 256
        wout = P.sb("wout", [128, NJ * D], BF16)
        wsum = P.sb("wsum", [16, D], F32)
        wfull = P.sb("wfull", [128, NJ * D], F32)
        blki = P.sb("blki", [128, NJ * 16], F32)
        hssb = P.sb("hssb", [16, T], F32)
        obrow = P.sb("obrow", [1, D], BF16)
        ones1 = P.sb("ones1", [1, 256], BF16)
        rest = [P.sb("rest", [128, 256], BF16) for _ in range(3)]
        a1t = [P.sb("a1t", [128, D], F32) for _ in range(3)]
        a2t = [P.sb("a2t", [128, D], F32) for _ in range(3)]
        h3t = [P.sb("h3t", [128, D], BF16) for _ in range(3)]
        h3b = [P.sb("h3b", [128, D], BF16) for _ in range(3)]
        gb = P.sb("gb", [128, D], F32)
        bb = P.sb("bb", [128, D], F32)
        grow = P.sb("grow", [1, D], F32)
        brow = P.sb("brow", [1, D], F32)
        onesf = P.sb("onesf", [1, 128], F32)
        scr = make_ln_scr(P, D)

        ld = P.sem("ld")
        gsem = P.sem("gsem")
        rlb = [P.sem(f"rlb{i}") for i in range(3)]
        alb = [P.sem(f"alb{i}") for i in range(3)]
        stb = [P.sem(f"stb{i}") for i in range(3)]
        mms = P.sem("mms")
        omm = P.sem("omm")
        evs = P.sem("evs")
        vsem = P.sem("vs")
        ssem = P.sem("ss")
        bcs = P.sem("bcs")

        with nc.Block() as blk:
            @blk.sync
            def _(eng):
                eng.dma_start(grow[:, :], P.dram["L0_attn_ln3_g"].ap().unsqueeze(0)).then_inc(gsem, 16)
                eng.dma_start(brow[:, :], P.dram["L0_attn_ln3_b"].ap().unsqueeze(0)).then_inc(gsem, 16)
                eng.dma_start(wout.ap().rearrange("p (f n) -> p f n", f=NJ),
                              P.dram["cb_L0_attn_out_w"].ap().rearrange("(f p) n -> p f n", p=128)
                              ).then_inc(ld, 16)
                eng.dma_start(obrow[:, :], P.dram["cb_L0_attn_out_b"].ap().unsqueeze(0)).then_inc(ld, 16)
                eng.dma_start(wfull.ap().rearrange("p (f n) -> p f n", f=NJ),
                              P.dram["L0_attn_out_w"].ap().rearrange("(f p) n -> p f n", p=128)
                              ).then_inc(ld, 16)
                eng.dma_start(blki.ap().rearrange("p (f n) -> p f n", f=NJ),
                              P.dram["c_blockind"].ap().rearrange("(f p) n -> p f n", p=128)
                              ).then_inc(ld, 16)
                eng.dma_start(hssb[:, :], P.dram["hs"][:, :]).then_inc(ld, 16)

            @blk.vector
            def _(eng):
                eng.memset(ones1[:, :], 1.0)
                eng.memset(onesf[:, :], 1.0)
                eng.memset(scr["junk"][:, :], 0.0)

        with contextlib.ExitStack() as sub:
            psB = sub.enter_context(nc.psum_tensor("psBf", [128, 512], F32))
            psW = sub.enter_context(nc.psum_tensor("psWf", [16, 1024], F32))
            with nc.Block() as blk:
                @blk.tensor
                def _(eng):
                    eng.wait_ge(gsem, 32)
                    k = 0
                    for srcrow in (grow, brow):
                        for jj in range(2):
                            if k > 0:
                                eng.wait_ge(bcs, k)
                            eng.matmul(psB[:, :], onesf[0:1, 0:128],
                                       srcrow[:, jj * 512:(jj + 1) * 512],
                                       start=True, stop=True).then_inc(bcs, 1)
                            k += 2
                    eng.wait_ge(ld, 16 * 5)
                    mmi = None
                    for nh in range(2):
                        for f in range(NJ):
                            mmi = eng.matmul(psW[:, nh * 512:(nh + 1) * 512],
                                             blki[:, f * 16:(f + 1) * 16],
                                             wfull[:, f * D + nh * 512: f * D + (nh + 1) * 512],
                                             start=(f == 0), stop=(f == NJ - 1))
                    mmi.then_inc(bcs, 1)

                @blk.vector
                def _(eng):
                    k = 0
                    for dst in (gb, bb):
                        for jj in range(2):
                            eng.wait_ge(bcs, k + 1)
                            eng.tensor_copy(dst[:, jj * 512:(jj + 1) * 512], psB[:, :]).then_inc(bcs, 1)
                            k += 2
                    eng.wait_ge(bcs, 9)
                    eng.tensor_copy(wsum[:, :], psW[:, :]).then_inc(bcs, 1)

        psO = [P.ps("psO", [128, 1024], F32) for _ in range(2)]
        psH = [P.ps("psH", [128, 1024], F32) for _ in range(2)]
        with nc.Block() as blk:
            @blk.sync
            def _(eng):
                def load_a(tw):
                    b = tw % 3
                    if tw >= 3:
                        eng.wait_ge(evs, tw - 2)
                        eng.wait_ge(stb[b], 64 * ((tw - 3) // 3 + 1))
                    eng.dma_start(a1t[b][:, :],
                                  P.dram["a1"][tw * 256:tw * 256 + 128, :]).then_inc(alb[b], 16)
                    eng.dma_start(a2t[b][:, :],
                                  P.dram["a1"][tw * 256 + 128:tw * 256 + 256, :]).then_inc(alb[b], 16)

                def store(tw):
                    eng.wait_ge(evs, tw + 1)
                    b = tw % 3
                    eng.dma_start(P.dram["a2"][tw * 256:tw * 256 + 128, :],
                                  a1t[b][:, :]).then_inc(stb[b], 16)
                    eng.dma_start(P.dram["a2"][tw * 256 + 128:tw * 256 + 256, :],
                                  a2t[b][:, :]).then_inc(stb[b], 16)
                    eng.dma_start(P.dram["h3tm"][tw * 256:tw * 256 + 128, :],
                                  h3t[b][:, 0:D]).then_inc(stb[b], 16)
                    eng.dma_start(P.dram["h3tm"][tw * 256 + 128:tw * 256 + 256, :],
                                  h3b[b][:, 0:D]).then_inc(stb[b], 16)

                load_a(0)
                load_a(1)
                for tw in range(NW):
                    for jj in range(NJ):
                        n = tw * NJ + jj
                        if n >= 3:
                            eng.wait_ge(omm, n - 2)
                        eng.dma_start(rest[n % 3][:, :],
                                      P.dram["resfm"][jj * 128:(jj + 1) * 128,
                                                      tw * 256:(tw + 1) * 256]
                                      ).then_inc(rlb[n % 3], 16)
                    if tw >= 1:
                        store(tw - 1)
                    if tw + 2 < NW:
                        load_a(tw + 2)
                store(NW - 1)
                for b in range(3):
                    eng.wait_ge(stb[b], 64 * len([t for t in range(NW) if t % 3 == b]))

            @blk.tensor
            def _(eng):
                eng.wait_ge(bcs, 10)
                for tw in range(NW):
                    if tw >= 1:
                        eng.wait_ge(evs, tw)
                    for sub_i in range(2):
                        for nh in range(2):
                            eng.matmul(psH[sub_i][:, nh * 512:(nh + 1) * 512],
                                       hssb[:, tw * 256 + sub_i * 128: tw * 256 + (sub_i + 1) * 128],
                                       wsum[:, nh * 512:(nh + 1) * 512],
                                       start=True, stop=True)
                    mmi = None
                    for jj in range(NJ):
                        n = tw * NJ + jj
                        eng.wait_ge(rlb[n % 3], 16 * (n // 3 + 1))
                        for sub_i in range(2):
                            for nh in range(2):
                                mmi = eng.matmul(
                                    psO[sub_i][:, nh * 512:(nh + 1) * 512],
                                    rest[n % 3][:, sub_i * 128:(sub_i + 1) * 128],
                                    wout[:, jj * D + nh * 512: jj * D + (nh + 1) * 512],
                                    start=(jj == 0), stop=False)
                        mmi.then_inc(omm, 1)
                    for sub_i in range(2):
                        for nh in range(2):
                            mmi = eng.matmul(psO[sub_i][:, nh * 512:(nh + 1) * 512],
                                             ones1[0:1, 0:128],
                                             obrow[:, nh * 512:(nh + 1) * 512],
                                             start=False, stop=True)
                    mmi.then_inc(mms, 1)

            @blk.vector
            def _(eng):
                vv, sv = 0, 0
                for tw in range(NW):
                    eng.wait_ge(mms, tw + 1)
                    eng.wait_ge(alb[tw % 3], 32 * (tw // 3 + 1))
                    if tw >= 3:
                        eng.wait_ge(stb[tw % 3], 64 * ((tw - 3) // 3 + 1))
                    b = tw % 3
                    eng.tensor_tensor(a1t[b][:, :], psO[0][:, :], a1t[b][:, :], OP.add)
                    eng.tensor_tensor(a1t[b][:, :], psH[0][:, :], a1t[b][:, :], OP.add)
                    eng.tensor_tensor(a2t[b][:, :], psO[1][:, :], a2t[b][:, :], OP.add)
                    eng.tensor_tensor(a2t[b][:, :], psH[1][:, :], a2t[b][:, :], OP.add)
                    vv, sv = ln_vec(eng, scr, a1t[b][:, :], h3t[b][:, 0:D], D,
                                    vsem, ssem, vv, sv, gb=gb, bb=bb)
                    vv, sv = ln_vec(eng, scr, a2t[b][:, :], h3b[b][:, 0:D], D,
                                    vsem, ssem, vv, sv, gb=gb, bb=bb)
                    eng.tensor_copy(scr["junk"][:, 0:1], scr["junk"][:, 0:1]).then_inc(evs, 1)

            @blk.scalar
            def _(eng):
                vv, sv = 0, 0
                for tw in range(NW):
                    vv, sv = ln_sca(eng, scr, vsem, ssem, vv, sv)
                    vv, sv = ln_sca(eng, scr, vsem, ssem, vv, sv)

    return Stage("l0_ffnout", ins, outs, build)

def st_l0_mlp():
    ins = ["h3tm", "a2", "cb_L0_attn_fc1_w", "cb_L0_attn_fc1_b",
           "cb_L0_attn_fc2_w", "cb_L0_attn_fc2_b"]
    outs = {
        "ufm": ((4096, T), ml_dtypes.bfloat16),
        "s": ((T, D), np.float32),
    }

    def build(P):
        nc = P.nc
        junk = P.sb("junk", [128, 8], F32)
        ones512 = P.sb("ones512", [1, 512], BF16)

        with contextlib.ExitStack() as sub:
            h3fm = sub.enter_context(nc.sbuf_tensor("h3fm", [128, DC * T], BF16))
            w1 = sub.enter_context(nc.sbuf_tensor("w1", [128, DC * 4096], BF16))
            b1row = sub.enter_context(nc.sbuf_tensor("b1row", [1, 4096], BF16))
            z2 = [sub.enter_context(nc.sbuf_tensor(f"z2_{i}", [128, 512], F32)) for i in range(2)]
            u1 = sub.enter_context(nc.sbuf_tensor("u1", [128, 512], F32))
            u2 = [sub.enter_context(nc.sbuf_tensor(f"u2_{i}", [128, 512], F32)) for i in range(2)]
            u3 = [sub.enter_context(nc.sbuf_tensor(f"u3_{i}", [128, 512], F32)) for i in range(2)]
            u4 = sub.enter_context(nc.sbuf_tensor("u4", [128, 512], F32))
            stg = [sub.enter_context(nc.sbuf_tensor(f"ustg_{i}", [128, 512], BF16)) for i in range(3)]
            psA = [sub.enter_context(nc.psum_tensor(f"psA_{i}", [128, 512], F32)) for i in range(2)]
            ld = P.sem("ld")
            mms = P.sem("mms")
            g1 = P.sem("g1")
            g2 = P.sem("g2")
            g3 = P.sem("g3")
            gdn = P.sem("gdn")
            stb = [P.sem(f"stbm{i}") for i in range(3)]
            tiles = [(ft, tw) for ft in range(32) for tw in range(T // 512)]

            with nc.Block() as blk:
                @blk.sync
                def _(eng):
                    for kc in range(DC):
                        eng.dma_start(h3fm[:, kc * T:(kc + 1) * T],
                                      P.dram["h3tm"][:, kc * 128:(kc + 1) * 128],
                                      transpose=True).then_inc(ld, 16)
                    eng.dma_start(w1.ap().rearrange("p (kc n) -> p kc n", kc=DC),
                                  P.dram["cb_L0_attn_fc1_w"].ap()
                                  .rearrange("(kc p) n -> p kc n", p=128)).then_inc(ld, 16)
                    eng.dma_start(b1row[:, :],
                                  P.dram["cb_L0_attn_fc1_b"].ap().unsqueeze(0)).then_inc(ld, 16)

                @blk.vector
                def _(eng):
                    eng.memset(ones512[:, :], 1.0)
                    eng.memset(junk[:, :], 0.0)

            with nc.Block() as blk:
                @blk.tensor
                def _(eng):
                    eng.wait_ge(ld, 16 * (DC + 2))
                    for n, (ft, tw) in enumerate(tiles):
                        if n >= 2:
                            eng.wait_ge(gdn, n - 1)
                        for kc in range(DC):
                            eng.matmul(psA[n % 2][:, :],
                                       w1[:, kc * 4096 + ft * 128: kc * 4096 + (ft + 1) * 128],
                                       h3fm[:, kc * T + tw * 512: kc * T + tw * 512 + 512],
                                       start=(kc == 0), stop=False)
                        eng.matmul(psA[n % 2][:, :], b1row[:, ft * 128:(ft + 1) * 128],
                                   ones512[:, :], start=False, stop=True).then_inc(mms, 1)

                @blk.scalar
                def _(eng):
                    for n, _t in enumerate(tiles):
                        eng.wait_ge(mms, n + 1)
                        eng.activation(z2[n % 2][:, :], psA[n % 2][:, :],
                                       AF.Square).then_inc(g1, 1)
                        eng.wait_ge(g2, n + 1)
                        eng.activation(u3[n % 2][:, :], u2[n % 2][:, :], AF.Tanh,
                                       scale=GELU_C).then_inc(g3, 1)

                @blk.vector
                def _(eng):
                    for n, _t in enumerate(tiles):
                        eng.wait_ge(g1, n + 1)
                        eng.tensor_scalar(u1[:, :], z2[n % 2][:, :], GELU_A, 1.0,
                                          OP.mult, OP.add)
                        eng.tensor_tensor(u2[n % 2][:, :], psA[n % 2][:, :], u1[:, :],
                                          OP.mult).then_inc(g2, 1)
                        eng.wait_ge(g3, n + 1)
                        if n >= 3:
                            eng.wait_ge(stb[n % 3], 16 * ((n - 3) // 3 + 1))
                        eng.tensor_tensor(u4[:, :], psA[n % 2][:, :], u3[n % 2][:, :], OP.mult)
                        eng.tensor_tensor(u4[:, :], u4[:, :], psA[n % 2][:, :], OP.add)
                        eng.tensor_scalar(stg[n % 3][:, :], u4[:, :], 0.5, None,
                                          OP.mult).then_inc(gdn, 1)

                @blk.sync
                def _(eng):
                    for n, (ft, tw) in enumerate(tiles):
                        eng.wait_ge(gdn, n + 1)
                        eng.dma_start(P.dram["ufm"][ft * 128:(ft + 1) * 128,
                                                    tw * 512:tw * 512 + 512],
                                      stg[n % 3][:, :]).then_inc(stb[n % 3], 16)
                    for b in range(3):
                        eng.wait_ge(stb[b], 16 * len([t for t in range(len(tiles)) if t % 3 == b]))

        with contextlib.ExitStack() as sub:
            w2 = sub.enter_context(nc.sbuf_tensor("w2", [128, 32 * D], BF16))
            b2row = sub.enter_context(nc.sbuf_tensor("b2row", [1, D], BF16))
            ublk = [sub.enter_context(nc.sbuf_tensor(f"ublk_{i}", [128, 32 * 512], BF16))
                    for i in range(2)]
            a2t = [sub.enter_context(nc.sbuf_tensor(f"a2t_{i}", [128, D], F32)) for i in range(3)]
            st_ = [sub.enter_context(nc.sbuf_tensor(f"sst_{i}", [128, D], F32)) for i in range(3)]
            psB = [sub.enter_context(nc.psum_tensor(f"psB_{i}", [128, 1024], F32))
                   for i in range(2)]
            ld2 = P.sem("ld2")
            ubb = [P.sem(f"ubb{i}") for i in range(2)]
            alb = [P.sem(f"alb2{i}") for i in range(3)]
            mm2 = P.sem("mm2")
            ev2 = P.sem("ev2")
            st2 = [P.sem(f"st2{i}") for i in range(3)]

            with nc.Block() as blk:
                @blk.sync
                def _(eng):
                    eng.dma_start(w2.ap().rearrange("p (f n) -> p f n", f=32),
                                  P.dram["cb_L0_attn_fc2_w"].ap()
                                  .rearrange("(f p) n -> p f n", p=128)).then_inc(ld2, 16)
                    eng.dma_start(b2row[:, :],
                                  P.dram["cb_L0_attn_fc2_b"].ap().unsqueeze(0)).then_inc(ld2, 16)

            with nc.Block() as blk:
                @blk.sync
                def _(eng):
                    def load_u(tg):
                        if tg >= 2:
                            eng.wait_ge(mm2, 4 * (tg - 1))
                        for kc in range(32):
                            eng.dma_start(ublk[tg % 2][:, kc * 512:(kc + 1) * 512],
                                          P.dram["ufm"][kc * 128:(kc + 1) * 128,
                                                        tg * 512:(tg + 1) * 512]
                                          ).then_inc(ubb[tg % 2], 16)

                    def load_a(n):
                        b = n % 3
                        if n >= 3:
                            eng.wait_ge(ev2, n - 2)
                        eng.dma_start(a2t[b][:, :],
                                      P.dram["a2"][n * 128:(n + 1) * 128, :]).then_inc(alb[b], 16)

                    def store(n):
                        eng.wait_ge(ev2, n + 1)
                        eng.dma_start(P.dram["s"][n * 128:(n + 1) * 128, :],
                                      st_[n % 3][:, :]).then_inc(st2[n % 3], 16)

                    load_u(0)
                    load_u(1)
                    load_a(0)
                    load_a(1)
                    for n in range(NT):
                        if n % 4 == 0 and n // 4 + 2 < 8:
                            load_u(n // 4 + 2)
                        if n >= 1:
                            store(n - 1)
                        if n + 2 < NT:
                            load_a(n + 2)
                    store(NT - 1)
                    for b in range(3):
                        eng.wait_ge(st2[b], 16 * len([t for t in range(NT) if t % 3 == b]))

                @blk.tensor
                def _(eng):
                    eng.wait_ge(ld2, 32)
                    for tg in range(8):
                        eng.wait_ge(ubb[tg % 2], 16 * 32 * (tg // 2 + 1))
                        for sub_i in range(4):
                            n = tg * 4 + sub_i
                            if n >= 2:
                                eng.wait_ge(ev2, n - 1)
                            for nh in range(2):
                                for kc in range(32):
                                    eng.matmul(psB[n % 2][:, nh * 512:(nh + 1) * 512],
                                               ublk[tg % 2][:, kc * 512 + sub_i * 128:
                                                            kc * 512 + (sub_i + 1) * 128],
                                               w2[:, kc * D + nh * 512: kc * D + (nh + 1) * 512],
                                               start=(kc == 0), stop=False)
                                mmi = eng.matmul(psB[n % 2][:, nh * 512:(nh + 1) * 512],
                                                 ones512[0:1, 0:128],
                                                 b2row[:, nh * 512:(nh + 1) * 512],
                                                 start=False, stop=True)
                            mmi.then_inc(mm2, 1)

                @blk.vector
                def _(eng):
                    for n in range(NT):
                        eng.wait_ge(mm2, n + 1)
                        eng.wait_ge(alb[n % 3], 16 * (n // 3 + 1))
                        if n >= 3:
                            eng.wait_ge(st2[n % 3], 16 * ((n - 3) // 3 + 1))
                        eng.tensor_tensor(st_[n % 3][:, :], psB[n % 2][:, :],
                                          a2t[n % 3][:, :], OP.add).then_inc(ev2, 1)

    return Stage("l0_mlp", ins, outs, build)



def st_l0_gate():
    ins = ["xqtm", "L0_ffn_w_gate", "L0_ffn_w_up", "L0_ffn_w_down", "s", "x"]
    outs = {
        "x1": ((T, D), np.float32),
        "xn1tm": ((T, D), ml_dtypes.bfloat16),
        "xn1lo": ((T, D), ml_dtypes.bfloat16),
    }
    DI = 2816
    NGC = [(0, 512), (512, 512), (1024, 512), (1536, 512), (2048, 512), (2560, 256)]

    def build(P):
        nc = P.nc
        wqg_d = nc.dram_tensor("wqg_scr", [DI, D], BF16)
        wqu_d = nc.dram_tensor("wqu_scr", [DI, D], BF16)
        wqd_d = nc.dram_tensor("wqd_scr", [D, DI], BF16)
        xqfm_d = nc.dram_tensor("xqfm_scr", [D, T], BF16)
        hq_d = nc.dram_tensor("hq_scr", [T, DI], BF16)

        with contextlib.ExitStack() as sub:
            def SB(n, s, d):
                return sub.enter_context(nc.sbuf_tensor(n, s, d))
            qscr = {}
            for n in ["an", "r1", "r2", "qm", "s1m", "s2m", "d1", "d2", "qv", "sgs", "ax"]:
                qscr[n] = SB("qA" + n, [128, 1024], F32)
            for n in ["bmax", "scale", "rinv"]:
                qscr[n] = SB("qA" + n, [128, 64], F32)
            qscr["junk"] = SB("qAjunk", [128, 8], F32)
            wt = [SB(f"wtA{i}", [128, 1024], F32) for i in range(2)]
            wtb = [SB(f"wtbA{i}", [128, 1024], BF16) for i in range(3)]
            wt2 = [SB(f"wt2A{i}", [128, DI], F32) for i in range(2)]
            wt2b = [SB(f"wt2bA{i}", [128, DI], BF16) for i in range(3)]
            xqb = SB("xqbA", [128, T], BF16)
            ldAb = [P.sem(f"ldAb{i}") for i in range(2)]
            qdA = P.sem("qdA")
            stAb = [P.sem(f"stAb{i}") for i in range(3)]
            tpA = P.sem("tpA")

            with nc.Block() as blk:
                @blk.sync
                def _(eng):
                    n = 0
                    for wname, wd in (("L0_ffn_w_gate", None), ("L0_ffn_w_up", None)):
                        for ti in range(22):
                            if n >= 2:
                                eng.wait_ge(qdA, n - 1)
                            eng.dma_start(wt[n % 2][:, :],
                                          P.dram[wname][ti * 128:(ti + 1) * 128, :]
                                          ).then_inc(ldAb[n % 2], 16)
                            n += 1
                    for ti in range(8):
                        if n >= 2:
                            eng.wait_ge(qdA, n - 1)
                        eng.dma_start(wt2[n % 2][:, :],
                                      P.dram["L0_ffn_w_down"][ti * 128:(ti + 1) * 128, :]
                                      ).then_inc(ldAb[n % 2], 16)
                        n += 1
                    for kc in range(DC):
                        if kc > 0:
                            eng.wait_ge(tpA, 32 * kc)
                        eng.dma_start(xqb[:, :], P.dram["xqtm"][:, kc * 128:(kc + 1) * 128],
                                      transpose=True).then_inc(tpA, 16)
                        eng.wait_ge(tpA, 32 * kc + 16)
                        eng.dma_start(xqfm_d[kc * 128:(kc + 1) * 128, :],
                                      xqb[:, :]).then_inc(tpA, 16)

                @blk.vector
                def _(eng):
                    n = 0
                    for wi in range(2):
                        for ti in range(22):
                            eng.wait_ge(ldAb[n % 2], 16 * (n // 2 + 1))
                            if n >= 3:
                                eng.wait_ge(stAb[n % 3], 16 * ((n - 3) // 3 + 1))
                            emit_fp4_quant(eng, qscr, wt[n % 2][:, :], wtb[n % 3][:, :], 1024)
                            v_pad(eng, qscr["junk"])
                            v_pad(eng, qscr["junk"])
                            eng.tensor_copy(qscr["bmax"][:, 0:1],
                                            qscr["bmax"][:, 0:1]).then_inc(qdA, 1)
                            n += 1
                    for ti in range(8):
                        eng.wait_ge(ldAb[n % 2], 16 * (n // 2 + 1))
                        if n >= 3:
                            eng.wait_ge(stAb[n % 3], 16 * ((n - 3) // 3 + 1))
                        for off2, cw2 in ((0, 1024), (1024, 1024), (2048, 768)):
                            emit_fp4_quant(eng, qscr, wt2[n % 2][:, off2:off2 + cw2],
                                           wt2b[n % 3][:, off2:off2 + cw2], cw2)
                        v_pad(eng, qscr["junk"])
                        v_pad(eng, qscr["junk"])
                        eng.tensor_copy(qscr["bmax"][:, 0:1],
                                        qscr["bmax"][:, 0:1]).then_inc(qdA, 1)
                        n += 1

                @blk.scalar
                def _(eng):
                    n = 0
                    for wd in (wqg_d, wqu_d):
                        for ti in range(22):
                            eng.wait_ge(qdA, n + 1)
                            eng.dma_start(wd[ti * 128:(ti + 1) * 128, :],
                                          wtb[n % 3][:, :]).then_inc(stAb[n % 3], 16)
                            n += 1
                    for ti in range(8):
                        eng.wait_ge(qdA, n + 1)
                        eng.dma_start(wqd_d[ti * 128:(ti + 1) * 128, :],
                                      wt2b[n % 3][:, :]).then_inc(stAb[n % 3], 16)
                        n += 1
                    for b in range(3):
                        eng.wait_ge(stAb[b], 16 * len([t for t in range(52) if t % 3 == b]))

        with contextlib.ExitStack() as sub:
            def SB(n, s, d):
                return sub.enter_context(nc.sbuf_tensor(n, s, d))
            qscr = {}
            for n in ["an", "r1", "r2", "qm", "s1m", "s2m", "d1", "d2", "qv", "sgs", "ax"]:
                qscr[n] = SB("qB" + n, [128, 512], F32)
            for n in ["bmax", "scale", "rinv"]:
                qscr[n] = SB("qB" + n, [128, 32], F32)
            qscr["junk"] = SB("qBjunk", [128, 8], F32)
            wgT = SB("wgT", [128, DC * DI], BF16)
            wuT = SB("wuT", [128, DC * DI], BF16)
            xqs = [SB(f"xqs{i}", [128, DC * 128], BF16) for i in range(2)]
            sg = [SB(f"sg{i}", [128, 512], F32) for i in range(2)]
            hb = SB("hb", [128, 512], F32)
            hqs = [SB(f"hqs{i}", [128, 512], BF16) for i in range(3)]
            psG = [sub.enter_context(nc.psum_tensor(f"psG{i}", [128, 512], F32)) for i in range(2)]
            psU = [sub.enter_context(nc.psum_tensor(f"psU{i}", [128, 512], F32)) for i in range(2)]
            ldB = P.sem("ldB")
            xlBb = [P.sem(f"xlBb{i}") for i in range(2)]
            mmB = P.sem("mmB")
            slB = P.sem("slB")
            hqB = P.sem("hqB")
            stBb = [P.sem(f"stBb{i}") for i in range(3)]
            tiles = [(tt, gi) for tt in range(NT) for gi in range(len(NGC))]

            with nc.Block() as blk:
                @blk.sync
                def _(eng):
                    for kc in range(DC):
                        eng.dma_start(wgT[:, kc * DI:(kc + 1) * DI],
                                      wqg_d[:, kc * 128:(kc + 1) * 128],
                                      transpose=True).then_inc(ldB, 16)
                        eng.dma_start(wuT[:, kc * DI:(kc + 1) * DI],
                                      wqu_d[:, kc * 128:(kc + 1) * 128],
                                      transpose=True).then_inc(ldB, 16)
                    for tt in range(NT):
                        if tt >= 2:
                            eng.wait_ge(mmB, len(NGC) * (tt - 1))
                        for kc in range(DC):
                            eng.dma_start(xqs[tt % 2][:, kc * 128:(kc + 1) * 128],
                                          xqfm_d[kc * 128:(kc + 1) * 128,
                                                 tt * 128:(tt + 1) * 128]).then_inc(xlBb[tt % 2], 16)

                @blk.tensor
                def _(eng):
                    eng.wait_ge(ldB, 16 * 16)
                    for n, (tt, gi) in enumerate(tiles):
                        off, cw = NGC[gi]
                        if gi == 0:
                            eng.wait_ge(xlBb[tt % 2], 16 * DC * (tt // 2 + 1))
                        if n >= 2:
                            eng.wait_ge(hqB, n - 1)
                        for kc in range(DC):
                            eng.matmul(psG[n % 2][:, 0:cw],
                                       xqs[tt % 2][:, kc * 128:(kc + 1) * 128],
                                       wgT[:, kc * DI + off: kc * DI + off + cw],
                                       start=(kc == 0), stop=(kc == DC - 1))
                        mmi = None
                        for kc in range(DC):
                            mmi = eng.matmul(psU[n % 2][:, 0:cw],
                                             xqs[tt % 2][:, kc * 128:(kc + 1) * 128],
                                             wuT[:, kc * DI + off: kc * DI + off + cw],
                                             start=(kc == 0), stop=(kc == DC - 1))
                        mmi.then_inc(mmB, 1)

                @blk.scalar
                def _(eng):
                    for n, (tt, gi) in enumerate(tiles):
                        off, cw = NGC[gi]
                        eng.wait_ge(mmB, n + 1)
                        eng.activation(sg[n % 2][:, 0:cw], psG[n % 2][:, 0:cw],
                                       AF.Silu).then_inc(slB, 1)
                        if n >= 1:
                            po, pcw = NGC[tiles[n - 1][1]]
                            ptt = tiles[n - 1][0]
                            eng.wait_ge(hqB, n)
                            eng.dma_start(hq_d[ptt * 128:(ptt + 1) * 128, po:po + pcw],
                                          hqs[(n - 1) % 3][:, 0:pcw]).then_inc(stBb[(n - 1) % 3], 16)
                        if n == len(tiles) - 1:
                            eng.wait_ge(hqB, n + 1)
                            eng.dma_start(hq_d[tt * 128:(tt + 1) * 128, off:off + cw],
                                          hqs[n % 3][:, 0:cw]).then_inc(stBb[n % 3], 16)
                            for b in range(3):
                                eng.wait_ge(stBb[b],
                                            16 * len([t for t in range(len(tiles)) if t % 3 == b]))

                @blk.vector
                def _(eng):
                    for n, (tt, gi) in enumerate(tiles):
                        off, cw = NGC[gi]
                        eng.wait_ge(slB, n + 1)
                        if n >= 3:
                            eng.wait_ge(stBb[n % 3], 16 * ((n - 3) // 3 + 1))
                        eng.tensor_tensor(hb[:, 0:cw], sg[n % 2][:, 0:cw],
                                          psU[n % 2][:, 0:cw], OP.mult)
                        emit_fp4_quant(eng, qscr, hb[:, 0:cw], hqs[n % 3][:, 0:cw], cw)
                        v_pad(eng, qscr["junk"])
                        v_pad(eng, qscr["junk"])
                        eng.tensor_copy(qscr["bmax"][:, 0:1],
                                        qscr["bmax"][:, 0:1]).then_inc(hqB, 1)



        hqfm_d = nc.dram_tensor("hqfm_scr", [DI, T], BF16)
        with contextlib.ExitStack() as sub:
            tb = [sub.enter_context(nc.sbuf_tensor(f"tbC0{i}", [128, T], BF16))
                  for i in range(2)]
            tpC = P.sem("tpC")
            with nc.Block() as blk:
                @blk.sync
                def _(eng):
                    for kc in range(DI // 128):
                        if kc >= 2:
                            eng.wait_ge(tpC, 32 * (kc - 1))
                        eng.dma_start(tb[kc % 2][:, :], hq_d[:, kc * 128:(kc + 1) * 128],
                                      transpose=True).then_inc(tpC, 16)
                        eng.wait_ge(tpC, 32 * kc + 16)
                        eng.dma_start(hqfm_d[kc * 128:(kc + 1) * 128, :],
                                      tb[kc % 2][:, :]).then_inc(tpC, 16)
                    eng.wait_ge(tpC, 32 * (DI // 128))

        with contextlib.ExitStack() as sub:
            def SB(n, s, d):
                return sub.enter_context(nc.sbuf_tensor(n, s, d))
            NKC = DI // 128
            hqg = [SB(f"hqg{i}", [128, NKC * 512], BF16) for i in range(2)]
            wdT = SB("wdT", [128, NKC * D], BF16)
            gsb = [SB(f"gsb{i}", [128, D], F32) for i in range(2)]
            xt = [SB(f"xtC{i}", [128, D], F32) for i in range(3)]
            st_ = [SB(f"stC{i}", [128, D], F32) for i in range(3)]
            x1t = [SB(f"x1C{i}", [128, D], F32) for i in range(3)]
            xbt = [SB(f"xbC{i}", [128, D], BF16) for i in range(3)]
            xlt = [SB(f"xlC{i}", [128, D], BF16) for i in range(3)]
            scr = {
                "junk": SB("junkC", [128, 8], F32),
                "m": SB("lnmC", [128, 1], F32),
                "vv": SB("lnvC", [128, 1], F32),
                "sd": SB("lnsdC", [128, 1], F32),
                "rstd": SB("lnrC", [128, 1], F32),
                "xc": SB("lnxcC", [128, D], F32),
                "sq": SB("lnsqC", [128, D], F32),
                "nrm": SB("lnnC", [128, D], F32),
            }
            xn1f = SB("xn1fC", [128, D], F32)
            psD = [sub.enter_context(nc.psum_tensor(f"psD{i}", [128, 1024], F32))
                   for i in range(2)]
            ldC = P.sem("ldC")
            hlCb = [P.sem(f"hlCb{i}") for i in range(2)]
            alCb = [P.sem(f"alCb{i}") for i in range(3)]
            mmC = P.sem("mmC")
            slC = P.sem("slC")
            evC = P.sem("evC")
            vsC = P.sem("vsC")
            ssC = P.sem("ssC")
            stCb = [P.sem(f"stCb{i}") for i in range(3)]

            with nc.Block() as blk:
                @blk.sync
                def _(eng):
                    for kc in range(NKC):
                        eng.dma_start(wdT[:, kc * D:(kc + 1) * D],
                                      wqd_d[:, kc * 128:(kc + 1) * 128],
                                      transpose=True).then_inc(ldC, 16)

                    def load_hq(tg):
                        if tg >= 2:
                            eng.wait_ge(mmC, 4 * (tg - 1))
                        for kc in range(NKC):
                            eng.dma_start(hqg[tg % 2][:, kc * 512:(kc + 1) * 512],
                                          hqfm_d[kc * 128:(kc + 1) * 128,
                                                 tg * 512:(tg + 1) * 512]).then_inc(hlCb[tg % 2], 16)

                    def load_xs(tt):
                        b = tt % 3
                        if tt >= 3:
                            eng.wait_ge(evC, tt - 2)
                        eng.dma_start(xt[b][:, :],
                                      P.dram["x"][tt * 128:(tt + 1) * 128, :]).then_inc(alCb[b], 16)
                        eng.dma_start(st_[b][:, :],
                                      P.dram["s"][tt * 128:(tt + 1) * 128, :]).then_inc(alCb[b], 16)

                    def store(tt):
                        eng.wait_ge(evC, tt + 1)
                        b = tt % 3
                        eng.dma_start(P.dram["x1"][tt * 128:(tt + 1) * 128, :],
                                      x1t[b][:, :]).then_inc(stCb[b], 16)
                        eng.dma_start(P.dram["xn1tm"][tt * 128:(tt + 1) * 128, :],
                                      xbt[b][:, :]).then_inc(stCb[b], 16)
                        eng.dma_start(P.dram["xn1lo"][tt * 128:(tt + 1) * 128, :],
                                      xlt[b][:, :]).then_inc(stCb[b], 16)

                    load_hq(0)
                    load_hq(1)
                    load_xs(0)
                    load_xs(1)
                    for tt in range(NT):
                        if tt % 4 == 0 and tt // 4 + 2 < 8:
                            load_hq(tt // 4 + 2)
                        if tt >= 1:
                            store(tt - 1)
                        if tt + 2 < NT:
                            load_xs(tt + 2)
                    store(NT - 1)
                    for b in range(3):
                        eng.wait_ge(stCb[b], 48 * len([t for t in range(NT) if t % 3 == b]))

                @blk.tensor
                def _(eng):
                    eng.wait_ge(ldC, 16 * NKC)
                    for tg in range(8):
                        eng.wait_ge(hlCb[tg % 2], 16 * NKC * (tg // 2 + 1))
                        for sub_i in range(4):
                            tt = tg * 4 + sub_i
                            if tt >= 2:
                                eng.wait_ge(slC, tt - 1)
                            mmi = None
                            for nh in range(2):
                                for kc in range(NKC):
                                    mmi = eng.matmul(
                                        psD[tt % 2][:, nh * 512:(nh + 1) * 512],
                                        hqg[tg % 2][:, kc * 512 + sub_i * 128:
                                                    kc * 512 + (sub_i + 1) * 128],
                                        wdT[:, kc * D + nh * 512: kc * D + (nh + 1) * 512],
                                        start=(kc == 0), stop=(kc == NKC - 1))
                            mmi.then_inc(mmC, 1)

                @blk.scalar
                def _(eng):
                    vv, sv = 0, 0
                    for tt in range(NT):
                        eng.wait_ge(mmC, tt + 1)
                        eng.activation(gsb[tt % 2][:, :], psD[tt % 2][:, :],
                                       AF.Silu).then_inc(slC, 1)
                        vv, sv = ln_sca(eng, scr, vsC, ssC, vv, sv)

                @blk.vector
                def _(eng):
                    vv, sv = 0, 0
                    for tt in range(NT):
                        eng.wait_ge(slC, tt + 1)
                        eng.wait_ge(alCb[tt % 3], 32 * (tt // 3 + 1))
                        if tt >= 3:
                            eng.wait_ge(stCb[tt % 3], 48 * ((tt - 3) // 3 + 1))
                        b = tt % 3
                        eng.tensor_tensor(st_[b][:, :], gsb[tt % 2][:, :], st_[b][:, :],
                                          OP.mult)
                        eng.tensor_tensor(x1t[b][:, :], st_[b][:, :], xt[b][:, :], OP.add)
                        vv, sv = ln_vec(eng, scr, x1t[b][:, :], xn1f[:, :], D,
                                        vsC, ssC, vv, sv)
                        eng.tensor_copy(xbt[b][:, :], xn1f[:, :])
                        eng.tensor_tensor(xlt[b][:, :], xn1f[:, :], xbt[b][:, :],
                                          OP.subtract)
                        eng.tensor_copy(scr["junk"][:, 0:1], scr["junk"][:, 0:1]
                                        ).then_inc(evC, 1)


    return Stage("l0_gate", ins, outs, build)



def st_l1_attnproj():
    ins = ["xn1tm", "cb_L1_attn_W1_q", "cb_L1_attn_W2_q", "cb_L1_attn_W1_k",
           "cb_L1_attn_W2_k", "cb_L1_attn_Wv", "cb_L1_attn_bv", "c_ident_bf"]
    outs = {
        "q6tm": ((T, 128), ml_dtypes.bfloat16),
        "jk6tm": ((T, 128), ml_dtypes.bfloat16),
        "vaug1": ((T, 1040), ml_dtypes.bfloat16),
    }
    PAIRS = [(0, 1), (0, 2), (0, 3), (1, 2), (1, 3), (2, 3)]
    JPERM = [5, 4, 3, 2, 1, 0]
    JSIGN = [1.0, -1.0, 1.0, 1.0, -1.0, 1.0]
    W = 4112

    def build(P):
        nc = P.nc
        xfm = P.sb("xfm", [128, DC * W], BF16)
        w1qT = P.sb("w1qT", [128, 16 * 64], BF16)
        w2qT = P.sb("w2qT", [128, 16 * 64], BF16)
        w1kT = P.sb("w1kT", [128, 8 * 64], BF16)
        w2kT = P.sb("w2kT", [128, 8 * 64], BF16)
        wvT = P.sb("wvT", [128, DC * D], BF16)
        bvrow = P.sb("bvrow", [1, D], BF16)
        idn = P.sb("idn", [128, 128], BF16)
        ones1 = P.sb("ones1", [1, 128], BF16)
        wtmp = P.sb("wtmp", [64, 2048], BF16)
        wtmp2 = P.sb("wtmp2", [128, 1024], BF16)
        qa = P.sb("qa", [128, 64], F32)
        qb = P.sb("qb", [128, 64], F32)
        ka = P.sb("ka", [128, 64], F32)
        kb = P.sb("kb", [128, 64], F32)
        q6f = P.sb("q6f", [128, 96], F32)
        k6f = P.sb("k6f", [128, 96], F32)
        sq6 = P.sb("sq6", [128, 96], F32)
        n2 = P.sb("n2", [128, 32], F32)
        nsd = P.sb("nsd", [128, 32], F32)
        rcp = P.sb("rcp", [128, 32], F32)
        junk = P.sb("junk", [128, 8], F32)
        qstg = [P.sb("qstg", [128, 128], BF16) for _ in range(3)]
        jstg = [P.sb("jstg", [128, 128], BF16) for _ in range(3)]
        vstg = [P.sb("vstg", [128, 1040], BF16) for _ in range(3)]
        psQ = [P.ps("psQ", [128, 512], F32) for _ in range(2)]
        psV = [P.ps("psV", [128, 1024], F32) for _ in range(2)]
        psT = P.ps("psT", [128, 128], BF16)

        ld = P.sem("ld")
        msem = P.sem("msem")
        tpp = P.sem("tpp")
        tpe = P.sem("tpe")
        mmq = P.sem("mmq")
        mmv = P.sem("mmv")
        pkd = P.sem("pkd")
        psq = P.sem("psq")
        ssq = P.sem("ssq")
        stq = [P.sem(f"stq{i}") for i in range(3)]

        with nc.Block() as blk:
            @blk.vector
            def _(eng):
                for kc in range(DC):
                    eng.memset(xfm[:, kc * W:kc * W + 16], 0.0)
                eng.memset(ones1[:, :], 1.0)
                eng.memset(junk[:, :], 0.0)
                for s in vstg:
                    eng.memset(s[:, :], 1.0)
                for s in qstg + jstg:
                    eng.memset(s[:, :], 0.0)
                eng.tensor_copy(junk[:, 0:1], junk[:, 0:1]).then_inc(msem, 1)

            @blk.sync
            def _(eng):
                eng.wait_ge(msem, 1)
                for kc in range(DC):
                    eng.dma_start(xfm[:, kc * W + 16:kc * W + 16 + T],
                                  P.dram["xn1tm"][:, kc * 128:(kc + 1) * 128],
                                  transpose=True).then_inc(ld, 16)
                eng.dma_start(wtmp[:, :], P.dram["cb_L1_attn_W1_q"][:, :]).then_inc(ld, 16)
                eng.dma_start(idn[:, :], P.dram["c_ident_bf"][:, :]).then_inc(ld, 16)
                eng.dma_start(bvrow[:, :], P.dram["cb_L1_attn_bv"].ap().unsqueeze(0)).then_inc(ld, 16)

        def transpose_small(blk_tensor, blk_vector, src_sb, n_chunks, dst, base):
            pass

        with nc.Block() as blk:
            @blk.tensor
            def _(eng):
                eng.wait_ge(ld, 16 * (DC + 3))
                cnt = 0
                for ci in range(16):
                    if cnt >= 1:
                        eng.wait_ge(tpe, cnt)
                    eng.transpose(psT[:, 0:64], wtmp[0:64, ci * 128:(ci + 1) * 128],
                                  idn[0:64, 0:64]).then_inc(tpp, 1)
                    cnt += 1
                for wname, nch in (("cb_L1_attn_W2_q", 16), ("cb_L1_attn_W1_k", 8),
                                   ("cb_L1_attn_W2_k", 8)):
                    eng.wait_ge(ld, 16 * (DC + 4 + (0 if wname == "cb_L1_attn_W2_q" else
                                                    1 if wname == "cb_L1_attn_W1_k" else 2)))
                    for ci in range(nch):
                        eng.wait_ge(tpe, cnt)
                        eng.transpose(psT[:, 0:64], wtmp[0:64, ci * 128:(ci + 1) * 128],
                                      idn[0:64, 0:64]).then_inc(tpp, 1)
                        cnt += 1
                for kc in range(DC):
                    eng.wait_ge(ld, 16 * (DC + 6 + kc + 1))
                    for ci in range(8):
                        eng.wait_ge(tpe, cnt)
                        eng.transpose(psT[:, :], wtmp2[:, ci * 128:(ci + 1) * 128],
                                      idn[:, :]).then_inc(tpp, 1)
                        cnt += 1

            @blk.sync
            def _(eng):
                base = 16 * (DC + 3)
                eng.wait_ge(tpe, 16)
                eng.dma_start(wtmp[:, :], P.dram["cb_L1_attn_W2_q"][:, :]).then_inc(ld, 16)
                eng.wait_ge(tpe, 32)
                eng.dma_start(wtmp[0:64, 0:1024], P.dram["cb_L1_attn_W1_k"][:, :]).then_inc(ld, 16)
                eng.wait_ge(tpe, 40)
                eng.dma_start(wtmp[0:64, 0:1024], P.dram["cb_L1_attn_W2_k"][:, :]).then_inc(ld, 16)
                for kc in range(DC):
                    eng.wait_ge(tpe, 48 + kc * 8)
                    eng.dma_start(wtmp2[:, :],
                                  P.dram["cb_L1_attn_Wv"][kc * 128:(kc + 1) * 128, :]
                                  ).then_inc(ld, 16)

            @blk.vector
            def _(eng):
                cnt = 0
                for dst, nch in ((w1qT, 16), (w2qT, 16), (w1kT, 8), (w2kT, 8)):
                    for ci in range(nch):
                        eng.wait_ge(tpp, cnt + 1)
                        eng.tensor_copy(dst[:, ci * 64:(ci + 1) * 64],
                                        psT[:, 0:64]).then_inc(tpe, 1)
                        cnt += 1
                for kc in range(DC):
                    for ci in range(8):
                        eng.wait_ge(tpp, cnt + 1)
                        eng.tensor_copy(wvT[:, ci * D + kc * 128: ci * D + kc * 128 + 128],
                                        psT[:, :]).then_inc(tpe, 1)
                        cnt += 1

        with nc.Block() as blk:
            @blk.tensor
            def _(eng):
                for tt in range(NT):
                    if tt >= 2:
                        eng.wait_ge(pkd, tt - 1)
                    for half, wT in ((0, w1qT), (1, w2qT)):
                        for kc in range(DC):
                            eng.matmul(psQ[tt % 2][:, half * 64:(half + 1) * 64],
                                       xfm[:, kc * W + 16 + tt * 128: kc * W + 16 + (tt + 1) * 128],
                                       wT[:, kc * 64:(kc + 1) * 64],
                                       start=(kc == 0), stop=False)
                        for kc in range(DC):
                            eng.matmul(psQ[tt % 2][:, half * 64:(half + 1) * 64],
                                       xfm[:, kc * W + 15 + tt * 128: kc * W + 15 + (tt + 1) * 128],
                                       wT[:, (8 + kc) * 64:(9 + kc) * 64],
                                       start=False, stop=(kc == DC - 1))
                    for half, wT in ((2, w1kT), (3, w2kT)):
                        mmi = None
                        for kc in range(DC):
                            mmi = eng.matmul(psQ[tt % 2][:, half * 64:(half + 1) * 64],
                                             xfm[:, kc * W + 16 + tt * 128: kc * W + 16 + (tt + 1) * 128],
                                             w1kT[:, kc * 64:(kc + 1) * 64] if half == 2
                                             else w2kT[:, kc * 64:(kc + 1) * 64],
                                             start=(kc == 0), stop=(kc == DC - 1))
                    mmi.then_inc(mmq, 1)
                    for nh in range(2):
                        for kc in range(DC):
                            eng.matmul(psV[tt % 2][:, nh * 512:(nh + 1) * 512],
                                       xfm[:, kc * W + 16 + tt * 128: kc * W + 16 + (tt + 1) * 128],
                                       wvT[:, kc * D + nh * 512: kc * D + (nh + 1) * 512],
                                       start=(kc == 0), stop=False)
                        mmi = eng.matmul(psV[tt % 2][:, nh * 512:(nh + 1) * 512],
                                         ones1[:, :], bvrow[:, nh * 512:(nh + 1) * 512],
                                         start=False, stop=True)
                    mmi.then_inc(mmv, 1)

            @blk.vector
            def _(eng):
                for tt in range(NT):
                    eng.wait_ge(mmq, tt + 1)
                    if tt >= 3:
                        eng.wait_ge(stq[tt % 3], 48 * ((tt - 3) // 3 + 1))
                    eng.tensor_copy(qa[:, :], psQ[tt % 2][:, 0:64])
                    eng.tensor_copy(qb[:, :], psQ[tt % 2][:, 64:128])
                    eng.tensor_copy(ka[:, :], psQ[tt % 2][:, 128:192])
                    eng.tensor_copy(kb[:, :], psQ[tt % 2][:, 192:256])
                    for (dst, aa, bb2) in ((q6f, qa, qb), (k6f, ka, kb)):
                        a4 = aa.ap().rearrange("p (h c) -> p h c", c=4)
                        b4 = bb2.ap().rearrange("p (h c) -> p h c", c=4)
                        d6 = dst.ap().rearrange("p (h c) -> p h c", c=6)
                        s6 = sq6.ap().rearrange("p (h c) -> p h c", c=6)
                        for pi, (i, j) in enumerate(PAIRS):
                            eng.tensor_tensor(d6[:, :, pi], a4[:, :, i], b4[:, :, j], OP.mult)
                            eng.tensor_tensor(s6[:, :, pi], a4[:, :, j], b4[:, :, i], OP.mult)
                        eng.tensor_tensor(dst[:, :], dst[:, :], sq6[:, :], OP.subtract)
                    eng.tensor_tensor(sq6[:, :], q6f[:, :], q6f[:, :], OP.mult)
                    eng.tensor_reduce(n2[:, 0:16],
                                      sq6.ap().rearrange("p (h c) -> p h c", c=6), AX.X, OP.add)
                    eng.tensor_tensor(sq6[:, :], k6f[:, :], k6f[:, :], OP.mult)
                    eng.tensor_reduce(n2[:, 16:32],
                                      sq6.ap().rearrange("p (h c) -> p h c", c=6), AX.X, OP.add)
                    v_pad(eng, junk)
                    eng.tensor_copy(junk[:, 0:1], junk[:, 0:1]).then_inc(psq, 1)
                    eng.wait_ge(ssq, tt + 1)
                    eng.tensor_scalar(nsd[:, :], nsd[:, :], 1e-12, None, OP.max)
                    v_pad(eng, junk)
                    eng.reciprocal(rcp[:, :], nsd[:, :])
                    v_pad(eng, junk)
                    rq = rcp[:, 0:16].unsqueeze(-1).broadcast_to([128, 16, 6])
                    rk = rcp[:, 16:32].unsqueeze(-1).broadcast_to([128, 16, 6])
                    q63 = q6f.ap().rearrange("p (h c) -> p h c", c=6)
                    k63 = k6f.ap().rearrange("p (h c) -> p h c", c=6)
                    eng.tensor_tensor(q63, q63, rq, OP.mult)
                    eng.tensor_tensor(k63, k63, rk, OP.mult)
                    eng.tensor_copy(qstg[tt % 3][:, 0:96], q6f[:, :])
                    j3 = jstg[tt % 3][:, 0:96].rearrange("p (h c) -> p h c", c=6)
                    for c in range(6):
                        eng.tensor_scalar(j3[:, 0:16, c], k63[:, :, JPERM[c]], JSIGN[c],
                                          None, OP.mult)
                    eng.wait_ge(mmv, tt + 1)
                    for h in range(16):
                        eng.tensor_copy(vstg[tt % 3][:, h * 65:h * 65 + 64],
                                        psV[tt % 2][:, h * 64:(h + 1) * 64])
                    eng.tensor_copy(junk[:, 0:1], junk[:, 0:1]).then_inc(pkd, 1)

            @blk.scalar
            def _(eng):
                for tt in range(NT):
                    eng.wait_ge(psq, tt + 1)
                    eng.activation(nsd[:, :], n2[:, :], AF.Sqrt)
                    eng.activation(junk[0:16, 0:1], nsd[0:16, 0:1], AF.Sign).then_inc(ssq, 1)

            @blk.sync
            def _(eng):
                for tt in range(NT):
                    eng.wait_ge(pkd, tt + 1)
                    eng.dma_start(P.dram["q6tm"][tt * 128:(tt + 1) * 128, :],
                                  qstg[tt % 3][:, :]).then_inc(stq[tt % 3], 16)
                    eng.dma_start(P.dram["jk6tm"][tt * 128:(tt + 1) * 128, :],
                                  jstg[tt % 3][:, :]).then_inc(stq[tt % 3], 16)
                    eng.dma_start(P.dram["vaug1"][tt * 128:(tt + 1) * 128, :],
                                  vstg[tt % 3][:, :]).then_inc(stq[tt % 3], 16)
                for b in range(3):
                    eng.wait_ge(stq[b], 48 * len([t for t in range(NT) if t % 3 == b]))

    return Stage("l1_attnproj", ins, outs, build)



def st_l1_attn():
    ins = ["q6tm", "jk6tm", "vaug1", "c_triu_bf"]
    outs = {"attn1_tm": ((T, D), ml_dtypes.bfloat16)}
    SCALE = 6.0 ** (-0.5)

    def build(P):
        nc = P.nc
        vaug = P.sb("vaugsb", [128, NT * 1040], BF16)
        qfm = P.sb("qfm", [128, T], BF16)
        jkfm = P.sb("jkfm", [128, T], BF16)
        qhh = [P.sb("qhh", [8, T], BF16) for _ in range(2)]
        jkh = [P.sb("jkh", [8, T], BF16) for _ in range(2)]
        triu = P.sb("triu", [128, 128], BF16)
        pt = [P.sb("pt", [128, 512], BF16) for _ in range(2)]
        attn = P.sb("attn", [128, NT * D], BF16)
        rl = P.sb("rl", [128, 1], F32)
        junk = P.sb("junk", [128, 8], F32)
        psS = [P.ps("psS", [128, 512], F32) for _ in range(2)]
        psO = [P.ps("psO", [128, 512], F32) for _ in range(4)]

        vg = P.sem("vg")
        tps = P.sem("tps")
        qks = [P.sem(f"qks{i}") for i in range(2)]
        hs = P.sem("hs")
        mmS = P.sem("mmS")
        exS = P.sem("exS")
        mkS = P.sem("mkS")
        pvS = P.sem("pvS")
        oeS = P.sem("oeS")
        sst = P.sem("sst")

        sched = []
        for h in range(16):
            for qg in range(8):
                for kc in range(4 * (qg + 1)):
                    sched.append((h, qg, kc))

        with nc.Block() as blk:
            @blk.sync
            def _(eng):
                eng.dma_start(triu[:, :], P.dram["c_triu_bf"][:, :]).then_inc(vg, 16)
                for tt in range(NT):
                    eng.dma_start(vaug[:, tt * 1040:(tt + 1) * 1040],
                                  P.dram["vaug1"][tt * 128:(tt + 1) * 128, :]).then_inc(vg, 16)
                eng.dma_start(qfm[:, :], P.dram["q6tm"][:, :], transpose=True).then_inc(tps, 16)
                eng.dma_start(jkfm[:, :], P.dram["jk6tm"][:, :], transpose=True).then_inc(tps, 16)
                eng.wait_ge(tps, 32)
                for h in range(16):
                    if h >= 2:
                        eng.wait_ge(hs, h - 1)
                    eng.dma_start(qhh[h % 2][0:6, :], qfm[6 * h:6 * h + 6, :]
                                  ).then_inc(qks[h % 2], 16)
                    eng.dma_start(jkh[h % 2][0:6, :], jkfm[6 * h:6 * h + 6, :]
                                  ).then_inc(qks[h % 2], 16)

            @blk.tensor
            def _(eng):
                nmm = 0
                for n, (h, qg, kc) in enumerate(sched):
                    last_in_grp = (kc == 4 * (qg + 1) - 1)
                    if kc == 0 and qg == 0:
                        if h == 0:
                            eng.wait_ge(vg, 16 * (1 + NT))
                        eng.wait_ge(qks[h % 2], 32 * (h // 2 + 1))
                    if nmm >= 2:
                        eng.wait_ge(exS, nmm - 1)
                    eng.matmul(psS[nmm % 2][:, :],
                               jkh[h % 2][0:6, kc * 128:(kc + 1) * 128],
                               qhh[h % 2][0:6, qg * 512:(qg + 1) * 512],
                               start=True, stop=True).then_inc(mmS, 1)
                    nmm += 1
                    gi = h * 8 + qg
                    if gi >= 1 and kc == 0:
                        eng.wait_ge(oeS, gi)
                    eng.wait_ge(mkS, nmm)
                    last_mm = None
                    for qt in range(4):
                        if kc > 4 * qg + qt:
                            continue
                        last_mm = eng.matmul(
                            psO[qt][:, 0:65],
                            pt[nmm % 2][:, qt * 128:(qt + 1) * 128],
                            vaug[:, kc * 1040 + h * 65: kc * 1040 + (h + 1) * 65],
                            start=(kc == 0), stop=last_in_grp)
                    last_mm.then_inc(pvS, 1)

            @blk.scalar
            def _(eng):
                for n, (h, qg, kc) in enumerate(sched):
                    eng.wait_ge(mmS, n + 1)
                    if n >= 2:
                        eng.wait_ge(pvS, n - 1)
                    eng.activation(pt[(n + 1) % 2][:, :], psS[n % 2][:, :], AF.Exp,
                                   scale=SCALE).then_inc(exS, 1)

            @blk.vector
            def _(eng):
                for n, (h, qg, kc) in enumerate(sched):
                    last_in_grp = (kc == 4 * (qg + 1) - 1)
                    eng.wait_ge(exS, n + 1)
                    did = False
                    for qt in range(4):
                        if kc == 4 * qg + qt:
                            eng.tensor_tensor(pt[(n + 1) % 2][:, qt * 128:(qt + 1) * 128],
                                              pt[(n + 1) % 2][:, qt * 128:(qt + 1) * 128],
                                              triu[:, :], OP.mult)
                            did = True
                    if not did:
                        v_pad(eng, junk)
                    eng.tensor_copy(junk[:, 0:1], junk[:, 0:1]).then_inc(mkS, 1)
                    if last_in_grp:
                        gi = h * 8 + qg
                        eng.wait_ge(pvS, n + 1)
                        for qt in range(4):
                            qtile = 4 * qg + qt
                            eng.reciprocal(rl[:, :], psO[qt][:, 64:65])
                            v_pad(eng, junk)
                            eng.tensor_scalar(
                                attn[:, qtile * D + h * 64: qtile * D + (h + 1) * 64],
                                psO[qt][:, 0:64], rl[:, :], None, OP.mult)
                        eng.tensor_copy(junk[:, 0:1], junk[:, 0:1]).then_inc(oeS, 1)
                        if qg == 7:
                            eng.tensor_copy(junk[:, 0:1], junk[:, 0:1]).then_inc(hs, 1)

        with nc.Block() as blk:
            @blk.sync
            def _(eng):
                for tt in range(NT):
                    eng.dma_start(P.dram["attn1_tm"][tt * 128:(tt + 1) * 128, :],
                                  attn[:, tt * D:(tt + 1) * D]).then_inc(sst, 16)
                eng.wait_ge(sst, 16 * NT)

    return Stage("l1_attn", ins, outs, build)



def st_l1_oproj():
    ins = ["attn1_tm", "cb_L1_attn_Wo", "cb_L1_attn_bo", "c_ident_bf"]
    outs = {"s1": ((T, D), np.float32)}

    def build(P):
        nc = P.nc
        afm = P.sb("afm", [128, DC * T], BF16)
        woT = P.sb("woT", [128, DC * D], BF16)
        wtmp = P.sb("wtmp", [128, D], BF16)
        idn = P.sb("idn", [128, 128], BF16)
        borow = P.sb("borow", [1, D], BF16)
        ones1 = P.sb("ones1", [1, 128], BF16)
        stg = [P.sb("stg", [128, D], F32) for _ in range(3)]
        psA = [P.ps("psA", [128, 1024], F32) for _ in range(2)]
        psT = P.ps("psT", [128, 128], BF16)

        ld = P.sem("ld")
        tpp = P.sem("tpp")
        tpe = P.sem("tpe")
        mms = P.sem("mms")
        evs = P.sem("evs")
        stb = [P.sem(f"stb{i}") for i in range(3)]

        with nc.Block() as blk:
            @blk.sync
            def _(eng):
                for kc in range(DC):
                    eng.dma_start(afm[:, kc * T:(kc + 1) * T],
                                  P.dram["attn1_tm"][:, kc * 128:(kc + 1) * 128],
                                  transpose=True).then_inc(ld, 16)
                eng.dma_start(idn[:, :], P.dram["c_ident_bf"][:, :]).then_inc(ld, 16)
                eng.dma_start(borow[:, :], P.dram["cb_L1_attn_bo"].ap().unsqueeze(0)).then_inc(ld, 16)
                for kc in range(DC):
                    eng.wait_ge(tpe, 8 * kc)
                    eng.dma_start(wtmp[:, :],
                                  P.dram["cb_L1_attn_Wo"][kc * 128:(kc + 1) * 128, :]
                                  ).then_inc(ld, 16)

            @blk.vector
            def _(eng):
                eng.memset(ones1[:, :], 1.0)
                cnt = 0
                for kc in range(DC):
                    for ci in range(8):
                        eng.wait_ge(tpp, cnt + 1)
                        eng.tensor_copy(woT[:, ci * D + kc * 128: ci * D + kc * 128 + 128],
                                        psT[:, :]).then_inc(tpe, 1)
                        cnt += 1

            @blk.tensor
            def _(eng):
                for kc in range(DC):
                    eng.wait_ge(ld, 16 * (DC + 2 + kc + 1))
                    for ci in range(8):
                        if kc or ci:
                            eng.wait_ge(tpe, 8 * kc + ci)
                        eng.transpose(psT[:, :], wtmp[:, ci * 128:(ci + 1) * 128],
                                      idn[:, :]).then_inc(tpp, 1)

        with nc.Block() as blk:
            @blk.tensor
            def _(eng):
                for tt in range(NT):
                    if tt >= 2:
                        eng.wait_ge(evs, tt - 1)
                    for nh in range(2):
                        for kc in range(DC):
                            eng.matmul(psA[tt % 2][:, nh * 512:(nh + 1) * 512],
                                       afm[:, kc * T + tt * 128: kc * T + (tt + 1) * 128],
                                       woT[:, kc * D + nh * 512: kc * D + (nh + 1) * 512],
                                       start=(kc == 0), stop=False)
                        mmi = eng.matmul(psA[tt % 2][:, nh * 512:(nh + 1) * 512],
                                         ones1[:, :], borow[:, nh * 512:(nh + 1) * 512],
                                         start=False, stop=True)
                    mmi.then_inc(mms, 1)

            @blk.vector
            def _(eng):
                for tt in range(NT):
                    eng.wait_ge(mms, tt + 1)
                    if tt >= 3:
                        eng.wait_ge(stb[tt % 3], 16 * ((tt - 3) // 3 + 1))
                    eng.tensor_copy(stg[tt % 3][:, :], psA[tt % 2][:, :]).then_inc(evs, 1)

            @blk.sync
            def _(eng):
                for tt in range(NT):
                    eng.wait_ge(evs, tt + 1)
                    eng.dma_start(P.dram["s1"][tt * 128:(tt + 1) * 128, :],
                                  stg[tt % 3][:, :]).then_inc(stb[tt % 3], 16)
                for b in range(3):
                    eng.wait_ge(stb[b], 16 * len([t for t in range(NT) if t % 3 == b]))

    return Stage("l1_oproj", ins, outs, build)



MOE_PHASES = 4
CS_MODE = 0
MOE_SET = {1, 2, 3, 4}
MOE_NEXP = 8
M3_MODE = 0

def st_l1_moe(part="a"):
    ins = ["xn1tm", "xn1lo", "x1", "s1", "L1_ffn_Wr", "L1_ffn_br",
           "cb_L1_ffn_W1", "cb_L1_ffn_b1", "cb_L1_ffn_W2", "cb_L1_ffn_b2", "c_ident_bf"]
    if part == "a":
        outs = {"xhfm": ((D, T), ml_dtypes.bfloat16),
                "macc": ((T, D), ml_dtypes.bfloat16),
                "gmtm": ((T, 8), np.float32),
                "gmtT": ((8, T), ml_dtypes.bfloat16)}
    else:
        ins = ins + ["xhfm", "macc", "gmtm", "gmtT"]
        outs = {"x2": ((T, D), np.float32)}

    def build(P):
        nc = P.nc
        if part == "a":
            xhfm_d = P.dram["xhfm"]
            macc_d = P.dram["macc"]
            xlfm_d = nc.dram_tensor("xlfm_scr", [D, T], BF16)
        else:
            xhfm_d = P.dram["xhfm"]
            macc_d = nc.dram_tensor("macc_scr2", [T, D], BF16)
        gm_all = P.sb("gm_all", [128, NT * 8], F32)
        gmT = P.sb("gmT", [8, T], BF16)
        junk = P.sb("junk", [128, 8], F32)
        gsm = P.sem("gsm")
        if part == "b":
            with nc.Block() as blk:
                @blk.sync
                def _(eng):
                    eng.dma_start(gm_all.ap().rearrange("p (nt c) -> p nt c", c=8),
                                  P.dram["gmtm"].ap().rearrange("(nt p) c -> p nt c", p=128)
                                  ).then_inc(gsm, 16)
                    eng.dma_start(gmT[:, :], P.dram["gmtT"][:, :]).then_inc(gsm, 16)
                    eng.dma_start(macc_d.ap(), P.dram["macc"].ap()).then_inc(gsm, 16)
                    eng.wait_ge(gsm, 48)

        if part == "b":
            MOE_E0, MOE_E1 = 4, MOE_NEXP
        else:
            MOE_E0, MOE_E1 = 0, min(4, MOE_NEXP)
        if part == "a":
          with contextlib.ExitStack() as sub:
            tb = [sub.enter_context(nc.sbuf_tensor(f"tbM{i}", [128, T], BF16)) for i in range(2)]
            tpM = P.sem("tpM")
            with nc.Block() as blk:
                @blk.sync
                def _(eng):
                    n = 0
                    for src_d, dst_d in (("xn1tm", xhfm_d), ("xn1lo", xlfm_d)):
                        for kc in range(DC):
                            if n >= 2:
                                eng.wait_ge(tpM, 32 * (n - 1))
                            eng.dma_start(tb[n % 2][:, :],
                                          P.dram[src_d][:, kc * 128:(kc + 1) * 128],
                                          transpose=True).then_inc(tpM, 16)
                            eng.wait_ge(tpM, 32 * n + 16)
                            eng.dma_start(dst_d[kc * 128:(kc + 1) * 128, :],
                                          tb[n % 2][:, :]).then_inc(tpM, 16)
                            n += 1
                    eng.wait_ge(tpM, 32 * n)

        if part == "a" and 2 in MOE_SET:
          with contextlib.ExitStack() as sub:
            def SB(n, s, d):
                return sub.enter_context(nc.sbuf_tensor(n, s, d))
            wr = SB("wr", [128, DC * 8], F32)
            wrhi = SB("wrhi", [128, DC * 8], BF16)
            wrlo = SB("wrlo", [128, DC * 8], BF16)
            wrtmp = SB("wrtmp", [128, DC * 8], F32)
            brrow = SB("brrow", [1, 8], F32)
            brb = SB("brb", [128, 8], F32)
            onesf = SB("onesfM", [1, 128], F32)
            idn = SB("idnM", [128, 128], BF16)
            xh = [SB(f"xhM{i}", [128, DC * 128], BF16) for i in range(2)]
            xl = [SB(f"xlM{i}", [128, DC * 128], BF16) for i in range(2)]
            lg = SB("lg", [128, 8], F32)
            m8 = SB("m8", [128, 1], F32)
            nm8 = SB("nm8", [128, 1], F32)
            ex8 = SB("ex8", [128, 8], F32)
            s8 = SB("s8", [128, 1], F32)
            ge = SB("ge", [128, 1], F32)
            msk = SB("msk", [128, 8], F32)
            gmb = [SB(f"gmb{i}", [128, 8], BF16) for i in range(2)]
            psR = [sub.enter_context(nc.psum_tensor(f"psR{i}", [128, 512], F32)) for i in range(2)]
            psG = sub.enter_context(nc.psum_tensor("psGm", [128, 512], F32))
            psTm = sub.enter_context(nc.psum_tensor("psTm", [8, 512], BF16))
            ldR = P.sem("ldR")
            wrd = P.sem("wrd")
            xlb = [P.sem(f"xlbR{i}") for i in range(2)]
            mmR = P.sem("mmR")
            vgd = P.sem("vgd")
            sgd = P.sem("sgd")
            gmd = P.sem("gmd")
            tpg = P.sem("tpg")

            with nc.Block() as blk:
                @blk.sync
                def _(eng):
                    eng.dma_start(wr.ap().rearrange("p (kc n) -> p kc n", kc=DC),
                                  P.dram["L1_ffn_Wr"].ap().rearrange("(kc p) n -> p kc n", p=128)
                                  ).then_inc(ldR, 16)
                    eng.dma_start(brrow[:, :], P.dram["L1_ffn_br"].ap().unsqueeze(0)).then_inc(ldR, 16)
                    eng.dma_start(idn[:, :], P.dram["c_ident_bf"][:, :]).then_inc(ldR, 16)

                @blk.vector
                def _(eng):
                    eng.memset(onesf[:, :], 1.0)
                    eng.memset(junk[:, :], 0.0)
                    eng.wait_ge(ldR, 48)
                    eng.tensor_copy(wrhi[:, :], wr[:, :])
                    v_pad(eng, junk)
                    eng.tensor_tensor(wrtmp[:, :], wr[:, :], wrhi[:, :], OP.subtract)
                    v_pad(eng, junk)
                    eng.tensor_copy(wrlo[:, :], wrtmp[:, :]).then_inc(wrd, 1)

            with nc.Block() as blk:
                @blk.tensor
                def _(eng):
                    eng.matmul(psG[:, 0:8], onesf[0:1, 0:128], brrow[:, :],
                               start=True, stop=True).then_inc(wrd, 1)

                @blk.vector
                def _(eng):
                    eng.wait_ge(wrd, 2)
                    eng.tensor_copy(brb[:, :], psG[:, 0:8]).then_inc(wrd, 1)

            with nc.Block() as blk:
                @blk.sync
                def _(eng):
                    for tt in range(NT):
                        if tt >= 2:
                            eng.wait_ge(mmR, tt - 1)
                        for kc in range(DC):
                            eng.dma_start(xh[tt % 2][:, kc * 128:(kc + 1) * 128],
                                          xhfm_d[kc * 128:(kc + 1) * 128,
                                                 tt * 128:(tt + 1) * 128]).then_inc(xlb[tt % 2], 16)
                            eng.dma_start(xl[tt % 2][:, kc * 128:(kc + 1) * 128],
                                          xlfm_d[kc * 128:(kc + 1) * 128,
                                                 tt * 128:(tt + 1) * 128]).then_inc(xlb[tt % 2], 16)

                @blk.tensor
                def _(eng):
                    for tt in range(NT):
                        eng.wait_ge(xlb[tt % 2], 16 * 2 * DC * (tt // 2 + 1))
                        if tt >= 2:
                            eng.wait_ge(vgd, 2 * (tt - 1))
                        mmi = None
                        for kc in range(DC):
                            for pa, pb in ((xh, wrhi), (xh, wrlo), (xl, wrhi)):
                                mmi = eng.matmul(psR[tt % 2][:, 0:8],
                                                 pa[tt % 2][:, kc * 128:(kc + 1) * 128],
                                                 pb[:, kc * 8:(kc + 1) * 8],
                                                 start=(kc == 0 and pa is xh and pb is wrhi),
                                                 stop=(kc == DC - 1 and pa is xl))
                        mmi.then_inc(mmR, 1)
                        eng.wait_ge(vgd, 2 * tt + 2)
                        eng.transpose(psTm[:, 0:128], gmb[tt % 2][:, :], idn[:, :]
                                      ).then_inc(tpg, 1)

                @blk.vector
                def _(eng):
                    for tt in range(NT):
                        eng.wait_ge(mmR, tt + 1)
                        eng.wait_ge(wrd, 3)
                        eng.tensor_tensor(lg[:, :], psR[tt % 2][:, 0:8], brb[:, :], OP.add)
                        eng.tensor_reduce(m8[:, :], lg[:, :], AX.X, OP.max)
                        v_pad(eng, junk)
                        eng.tensor_scalar(nm8[:, :], m8[:, :], -1.0, None, OP.mult)
                        v_pad(eng, junk)
                        eng.tensor_copy(junk[:, 0:1], junk[:, 0:1]).then_inc(vgd, 1)
                        eng.wait_ge(sgd, tt + 1)
                        eng.reciprocal(ge[:, :], s8[:, :])
                        eng.tensor_scalar(msk[:, :], lg[:, :], m8[:, :], None, OP.is_ge)
                        v_pad(eng, junk)
                        eng.tensor_scalar(gm_all[:, tt * 8:(tt + 1) * 8], msk[:, :],
                                          ge[:, :], None, OP.mult)
                        v_pad(eng, junk)
                        if tt >= 1:
                            eng.wait_ge(tpg, tt)
                            eng.tensor_copy(gmT[:, (tt - 1) * 128:tt * 128],
                                            psTm[0:8, 0:128]).then_inc(gmd, 1)
                        eng.tensor_copy(gmb[tt % 2][:, :],
                                        gm_all[:, tt * 8:(tt + 1) * 8]).then_inc(vgd, 1)
                    eng.wait_ge(tpg, NT)
                    eng.tensor_copy(gmT[:, (NT - 1) * 128:NT * 128],
                                    psTm[0:8, 0:128]).then_inc(gmd, 1)

                @blk.scalar
                def _(eng):
                    for tt in range(NT):
                        eng.wait_ge(vgd, 2 * tt + 1)
                        eng.activation(ex8[:, :], lg[:, :], AF.Exp, bias=nm8[:, :],
                                       accum_out=s8[:, :])
                        eng.activation(junk[0:16, 0:1], s8[0:16, 0:1], AF.Sign).then_inc(sgd, 1)

        if part == "a":
            gss = P.sem("gss")
            with nc.Block() as blk:
                @blk.sync
                def _(eng):
                    eng.dma_start(P.dram["gmtm"].ap().rearrange("(nt p) c -> p nt c", p=128),
                                  gm_all.ap().rearrange("p (nt c) -> p nt c", c=8)
                                  ).then_inc(gss, 16)
                    eng.dma_start(P.dram["gmtT"][:, :], gmT[:, :]).then_inc(gss, 16)
                    eng.wait_ge(gss, 32)

        if 3 not in MOE_SET:
            return
        for e in range(MOE_E0, MOE_E1):
            with contextlib.ExitStack() as sub:
                def SB(n, s, d, e=e):
                    return sub.enter_context(nc.sbuf_tensor(f"{n}x{e}", s, d))
                w1sb = SB("w1sb", [128, DC * 4096], BF16)
                w2sb = SB("w2sb", [128, 32 * D], BF16)
                b1row = SB("b1row", [1, 4096], BF16)
                ones256 = SB("ones256", [1, 256], BF16)
                xs = [SB(f"xsE{i}", [128, DC * 256], BF16) for i in range(2)]
                hsb = [SB(f"hsbE{i}", [128, 32 * 256], BF16) for i in range(2)]
                mt = [SB(f"mtE{i}", [128, 2 * D], BF16) for i in range(2)]
                stgE = [SB(f"stgE{i}", [128, 2 * D], BF16) for i in range(2)]
                psh = [sub.enter_context(nc.psum_tensor(f"pshE{e}_{i}", [128, 512], F32))
                       for i in range(2)]
                psy = [sub.enter_context(nc.psum_tensor(f"psyE{e}_{i}", [128, 1024], F32))
                       for i in range(2)]
                def SEM(n, e=e):
                    return sub.enter_context(nc.semaphore(f"{n}x{e}"))
                ldE = SEM("ldE")
                xlbE = [SEM(f"xlbE{i}") for i in range(2)]
                albE = [SEM(f"albE{i}") for i in range(2)]
                mmh = SEM("mmh")
                ghd = SEM("ghd")
                mmy = SEM("mmy")
                evy = SEM("evy")
                styE = [SEM(f"styE{i}") for i in range(2)]
                NTC = 16

                with nc.Block() as blk:
                    @blk.sync
                    def _(eng):
                        eng.dma_start(
                            w1sb.ap().rearrange("p (kc n) -> p kc n", kc=DC),
                            P.dram["cb_L1_ffn_W1"].ap()[e].rearrange("(kc p) n -> p kc n", p=128)
                        ).then_inc(ldE, 16)
                        eng.dma_start(
                            w2sb.ap().rearrange("p (f n) -> p f n", f=32),
                            P.dram["cb_L1_ffn_W2"].ap()[e].rearrange("(f p) n -> p f n", p=128)
                        ).then_inc(ldE, 16)
                        eng.dma_start(b1row[:, :],
                                      P.dram["cb_L1_ffn_b1"][e:e + 1, :]).then_inc(ldE, 16)

                    @blk.vector
                    def _(eng):
                        eng.memset(ones256[:, :], 1.0)

                with nc.Block() as blk:
                    @blk.sync
                    def _(eng):
                        def load(tc):
                            b = tc % 2
                            if tc >= 2:
                                eng.wait_ge(mmy, 2 * (tc - 1))
                                eng.wait_ge(evy, 2 * (tc - 1))
                            for kc in range(DC):
                                eng.dma_start(xs[b][:, kc * 256:(kc + 1) * 256],
                                              xhfm_d[kc * 128:(kc + 1) * 128,
                                                     tc * 256:(tc + 1) * 256]
                                              ).then_inc(xlbE[b], 16)
                            if e > 0:
                                eng.dma_start(mt[b][:, 0:D],
                                              macc_d[tc * 256:tc * 256 + 128, :]).then_inc(albE[b], 16)
                                eng.dma_start(mt[b][:, D:2 * D],
                                              macc_d[tc * 256 + 128:tc * 256 + 256, :]
                                              ).then_inc(albE[b], 16)

                        def store(tc):
                            b = tc % 2
                            eng.wait_ge(evy, 2 * (tc + 1))
                            eng.dma_start(macc_d[tc * 256:tc * 256 + 128, :],
                                          stgE[b][:, 0:D]).then_inc(styE[b], 16)
                            eng.dma_start(macc_d[tc * 256 + 128:tc * 256 + 256, :],
                                          stgE[b][:, D:2 * D]).then_inc(styE[b], 16)
                        load(0)
                        load(1)
                        for tc in range(NTC):
                            if tc >= 1:
                                store(tc - 1)
                            if tc + 2 < NTC:
                                load(tc + 2)
                        store(NTC - 1)
                        for b in range(2):
                            eng.wait_ge(styE[b], 32 * (NTC // 2))

                    @blk.tensor
                    def _(eng):
                        eng.wait_ge(ldE, 48)
                        for tc in range(NTC):
                            b = tc % 2
                            eng.wait_ge(xlbE[b], 16 * DC * (tc // 2 + 1))
                            for ft in range(32):
                                if tc > 0 or ft >= 2:
                                    eng.wait_ge(ghd, tc * 32 + ft - 1)
                                for kc in range(DC):
                                    eng.matmul(psh[ft % 2][:, 0:256],
                                               w1sb[:, kc * 4096 + ft * 128: kc * 4096 + (ft + 1) * 128],
                                               xs[b][:, kc * 256:(kc + 1) * 256],
                                               start=(kc == 0), stop=False)
                                eng.matmul(psh[ft % 2][:, 0:256],
                                           b1row[:, ft * 128:(ft + 1) * 128], ones256[:, :],
                                           start=False, stop=True).then_inc(mmh, 1)
                            eng.wait_ge(ghd, tc * 32 + 32)
                            if tc >= 1:
                                eng.wait_ge(evy, 2 * tc)
                            for sub_i in range(2):
                                mmi = None
                                for nh in range(2):
                                    for ft in range(32):
                                        mmi = eng.matmul(
                                            psy[sub_i][:, nh * 512:(nh + 1) * 512],
                                            hsb[b][:, ft * 256 + sub_i * 128: ft * 256 + sub_i * 128 + 128],
                                            w2sb[:, ft * D + nh * 512: ft * D + (nh + 1) * 512],
                                            start=(ft == 0), stop=(ft == 31))
                                mmi.then_inc(mmy, 1)

                    @blk.scalar
                    def _(eng):
                        for tc in range(NTC):
                            b = tc % 2
                            for ft in range(32):
                                eng.wait_ge(mmh, tc * 32 + ft + 1)
                                eng.activation(hsb[b][:, ft * 256:(ft + 1) * 256],
                                               psh[ft % 2][:, 0:256],
                                               AF.Gelu).then_inc(ghd, 1)

                    @blk.vector
                    def _(eng):
                        for tc in range(NTC):
                            b = tc % 2
                            for sub_i in range(2):
                                eng.wait_ge(mmy, 2 * tc + sub_i + 1)
                                if tc >= 2 and sub_i == 0:
                                    eng.wait_ge(styE[b], 32 * (tc // 2))
                                if e > 0:
                                    eng.wait_ge(albE[b], 32 * (tc // 2 + 1))
                                tti = tc * 2 + sub_i
                                gcol = gm_all[:, tti * 8 + e: tti * 8 + e + 1]
                                if e == 0:
                                    eng.tensor_scalar(stgE[b][:, sub_i * D:(sub_i + 1) * D],
                                                      psy[sub_i][:, :], gcol, None,
                                                      OP.mult).then_inc(evy, 1)
                                else:
                                    eng.scalar_tensor_tensor(
                                        stgE[b][:, sub_i * D:(sub_i + 1) * D],
                                        psy[sub_i][:, :], gcol,
                                        mt[b][:, sub_i * D:(sub_i + 1) * D],
                                        OP.mult, OP.add).then_inc(evy, 1)

        if part != "b" or 4 not in MOE_SET:
            return
        with contextlib.ExitStack() as sub:
            def SB(n, s, d):
                return sub.enter_context(nc.sbuf_tensor(n, s, d))
            b2sb = SB("b2sb", [8, D], BF16)
            mtF = [SB(f"mtF{i}", [128, D], BF16) for i in range(3)]
            x1F = [SB(f"x1F{i}", [128, D], F32) for i in range(3)]
            s1F = [SB(f"s1F{i}", [128, D], F32) for i in range(3)]
            g1F = [SB(f"g1F{i}", [128, D], F32) for i in range(2)]
            moF = [SB(f"moF{i}", [128, D], F32) for i in range(2)]
            x2F = [SB(f"x2F{i}", [128, D], F32) for i in range(3)]
            psF = [sub.enter_context(nc.psum_tensor(f"psF{i}", [128, 1024], F32))
                   for i in range(2)]
            ldF = P.sem("ldF")
            alF = [P.sem(f"alF{i}") for i in range(3)]
            mmF = P.sem("mmF")
            vmo = P.sem("vmo")
            sgF = P.sem("sgF")
            evF = P.sem("evF")
            stF = [P.sem(f"stF{i}") for i in range(3)]

            with nc.Block() as blk:
                @blk.sync
                def _(eng):
                    eng.dma_start(b2sb[:, :], P.dram["cb_L1_ffn_b2"][:, :]).then_inc(ldF, 16)

                    def load(tt):
                        b = tt % 3
                        if tt >= 3:
                            eng.wait_ge(evF, tt - 2)
                            eng.wait_ge(stF[b], 16 * ((tt - 3) // 3 + 1))
                        eng.dma_start(mtF[b][:, :],
                                      macc_d[tt * 128:(tt + 1) * 128, :]).then_inc(alF[b], 16)
                        eng.dma_start(x1F[b][:, :],
                                      P.dram["x1"][tt * 128:(tt + 1) * 128, :]).then_inc(alF[b], 16)
                        eng.dma_start(s1F[b][:, :],
                                      P.dram["s1"][tt * 128:(tt + 1) * 128, :]).then_inc(alF[b], 16)

                    def store(tt):
                        eng.wait_ge(evF, tt + 1)
                        eng.dma_start(P.dram["x2"][tt * 128:(tt + 1) * 128, :],
                                      x2F[tt % 3][:, :]).then_inc(stF[tt % 3], 16)
                    load(0)
                    load(1)
                    for tt in range(NT):
                        if tt + 2 < NT:
                            load(tt + 2)
                        store(tt)
                    for b in range(3):
                        eng.wait_ge(stF[b], 16 * len([t for t in range(NT) if t % 3 == b]))

                @blk.tensor
                def _(eng):
                    if M3_MODE == 1:
                        return
                    eng.wait_ge(ldF, 16)
                    for tt in range(NT):
                        if tt >= 2:
                            eng.wait_ge(vmo, tt - 1)
                        mmi = None
                        lh = b2sb[:, 0:128] if M3_MODE == 3 else gmT[:, tt * 128:(tt + 1) * 128]
                        for nh in range(2):
                            mmi = eng.matmul(psF[tt % 2][:, nh * 512:(nh + 1) * 512],
                                             lh,
                                             b2sb[:, nh * 512:(nh + 1) * 512],
                                             start=True, stop=True)
                        mmi.then_inc(mmF, 1)

                @blk.vector
                def _(eng):
                    for tt in range(NT):
                        b = tt % 3
                        if M3_MODE != 1:
                            eng.wait_ge(mmF, tt + 1)
                        eng.wait_ge(alF[b], 48 * (tt // 3 + 1))
                        if tt >= 3:
                            eng.wait_ge(stF[b], 16 * ((tt - 3) // 3 + 1))
                        if M3_MODE == 1:
                            eng.tensor_copy(moF[tt % 2][:, :], mtF[b][:, :])
                            eng.tensor_copy(junk[:, 0:1], junk[:, 0:1]).then_inc(vmo, 1)
                        else:
                            eng.tensor_tensor(moF[tt % 2][:, :], psF[tt % 2][:, :],
                                              mtF[b][:, :], OP.add).then_inc(vmo, 1)
                        if M3_MODE == 2:
                            eng.tensor_copy(x2F[b][:, :], moF[tt % 2][:, :])
                            eng.tensor_copy(junk[:, 0:1], junk[:, 0:1]).then_inc(evF, 1)
                        else:
                            eng.wait_ge(sgF, tt + 1)
                            eng.tensor_tensor(x2F[b][:, :], g1F[tt % 2][:, :], s1F[b][:, :],
                                              OP.mult)
                            eng.tensor_tensor(x2F[b][:, :], x2F[b][:, :], x1F[b][:, :],
                                              OP.add).then_inc(evF, 1)

                @blk.scalar
                def _(eng):
                    if M3_MODE == 2:
                        return
                    for tt in range(NT):
                        eng.wait_ge(vmo, tt + 1)
                        eng.activation(g1F[tt % 2][:, :], moF[tt % 2][:, :],
                                       AF.Silu).then_inc(sgF, 1)

    return Stage(f"l1_moe_{part}", ins, outs, build)
